# version 10
# baseline (speedup 1.0000x reference)
import numpy as np
from contextlib import ExitStack
import concourse.bass as bass
import concourse.mybir as mybir
from concourse.bass import IndirectOffsetOnAxis

F32 = mybir.dt.float32
BF16 = mybir.dt.bfloat16
I32 = mybir.dt.int32
U32 = mybir.dt.uint32
ALU = mybir.AluOpType
AF = mybir.ActivationFunctionType
AX = mybir.AxisListType

P = 128
TT = 32
DM = 1024
FC = 8
NH = 16
HD = 64
HFF = 4096
HC = 32
CAP = 4096
NEXP = 8
NCORE = 8
EPS = 1e-5
VW = 1040
WIA_W = 2048 + VW

NQ_PAIR = [64, 128, 256, 512, 512, 512, 512, 512]
SLOPE = [2.0 ** (-0.5 * (h + 1)) for h in range(NH)]
WIN_T = 45.0

ENGINES = ("sync", "tensor", "vector", "scalar", "gpsimd")


class MK:
    def __init__(self, stage="full"):
        self.stage = stage
        self.nc = bass.Bass("TRN2", target_bir_lowering=False)
        self.ops = {e: [] for e in ENGINES}
        self.SEMS = (["t", "v", "s", "cc", "bar", "st", "boot", "bootg", "bcr", "bcw", "bca",
                      "gt", "rt", "wq", "pl", "wol", "wf", "ds"]
                     + [f"{b}{i}" for b in ("lnl", "lns", "dl", "ds", "dg", "dst", "ft", "es",
                                            "rg", "rl", "rs", "ys", "xt", "qk", "vst", "pjl", "pjs", "syl")
                        for i in (0, 1)])
        self.cnt = {n: 0 for n in self.SEMS}
        self.sems = {}
        self.dbg_outs = []

    def q(self, eng, fn):
        self.ops[eng].append(fn)

    def emit(self, eng, sem=None, by=1, make=None):
        if sem is None:
            self.q(eng, make)
            return None
        self.cnt[sem] += by
        sems = self.sems
        self.q(eng, lambda e, make=make, sem=sem, by=by: make(e).then_inc(sems[sem], by))
        return self.cnt[sem]

    def wait(self, eng, sem, val=None):
        if val is None:
            val = self.cnt[sem]
        if val <= 0:
            return
        sems = self.sems
        self.q(eng, lambda e, sem=sem, val=val: e.wait_ge(sems[sem], val))

    def mark(self, eng, sem, by=1):
        self.cnt[sem] += by
        last = self.ops[eng].pop()
        sems = self.sems
        self.ops[eng].append(lambda e, last=last, sem=sem, by=by: last(e).then_inc(sems[sem], by))
        return self.cnt[sem]

    def dma(self, eng, out, in_, sem):
        return self.emit(eng, sem, 16, lambda e, out=out, in_=in_: e.dma_start(out=out, in_=in_))

    def dmat(self, eng, out, in_, sem):
        return self.emit(eng, sem, 16, lambda e, out=out, in_=in_: e.dma_start_transpose(out, in_))

    def barrier(self):
        for eng in ENGINES:
            for s in self.SEMS:
                if s != "bar" and self.cnt[s] > 0:
                    self.wait(eng, s, self.cnt[s])
            self.emit(eng, "bar", 1, lambda e: e.nop())
        tot = self.cnt["bar"]
        for eng in ENGINES:
            self.wait(eng, "bar", tot)

    def dbg(self, name, ap_src, shape, dtype=F32):
        t = self.nc.dram_tensor("DBG_" + name, list(shape), dtype, kind="ExternalOutput").ap()
        self.dbg_outs.append(name)
        self.dma("sync", t, ap_src, "st")

    def declare(self):
        nc = self.nc
        self.x_in = nc.dram_tensor("x_my", [TT * P, DM], F32, kind="ExternalInput").ap()
        self.w1_in = nc.dram_tensor("w1_my", [DM, HFF], F32, kind="ExternalInput").ap()
        self.w2_in = nc.dram_tensor("w2_my", [HFF, DM], F32, kind="ExternalInput").ap()
        self.b1_in = nc.dram_tensor("b1_my", [P, HC], F32, kind="ExternalInput").ap()
        self.b2_in = nc.dram_tensor("b2_my", [1, DM], F32, kind="ExternalInput").ap()
        self.wg_in = nc.dram_tensor("wg", [DM, NEXP], F32, kind="ExternalInput").ap()
        self.wiaT_in = nc.dram_tensor("wiaT", [DM, WIA_W], F32, kind="ExternalInput").ap()
        self.woT_in = nc.dram_tensor("woT", [DM, DM], F32, kind="ExternalInput").ap()
        self.y_out = nc.dram_tensor("y", [TT * P, DM], F32, kind="ExternalOutput").ap()

        dt = nc.dram_tensor
        self.resid = dt("resid", [P, TT, DM], BF16, kind="Internal").ap()
        self.xln_tm = dt("xln_tm", [TT * P, DM], BF16, kind="Internal").ap()
        self.compact = dt("compact", [CAP, DM], BF16, kind="Internal").ap()
        self.ag_x = dt("ag_x", [NCORE * CAP, DM], BF16, kind="Internal", addr_space="Shared").ap()
        self.disp_tm = dt("disp_tm", [CAP, DM], BF16, kind="Internal").ap()
        self.expout = dt("expout", [CAP, DM], BF16, kind="Internal").ap()
        self.ag_ret = dt("ag_ret", [NCORE * CAP, DM], BF16, kind="Internal", addr_space="Shared").ap()
        self.qk_fm = dt("qk_fm", [2048, TT * P], BF16, kind="Internal").ap()
        self.vaug = dt("vaug", [P, TT, VW], BF16, kind="Internal").ap()
        self.w1b = dt("w1b", [DM, HFF], BF16, kind="Internal").ap()
        self.w2b = dt("w2b", [HFF, DM], BF16, kind="Internal").ap()
        self.wiaTb = dt("wiaTb", [DM, WIA_W], BF16, kind="Internal").ap()
        self.woTb = dt("woTb", [DM, DM], BF16, kind="Internal").ap()
        self.escr = dt("escr", [TT, P], BF16, kind="Internal").ap()
        self.rowscr = dt("rowscr", [16, 1], F32, kind="Internal").ap()
        self.agc_in = dt("agc_in", [1, NEXP], F32, kind="Internal").ap()
        self.agc_out = dt("agc_out", [NCORE, NEXP], F32, kind="Internal", addr_space="Shared").ap()

        it = nc.inline_tensor
        self.c_ident = it(np.eye(P, dtype=np.float32), name="c_ident")
        jj = np.arange(1152)[None, :]
        pp = np.arange(P)[:, None]
        self.c_stair = it(np.where((jj - 512) < pp, -30000.0, 0.0).astype(np.float32), name="c_stair")
        self.c_lt8 = it(np.triu(np.ones((8, 8), dtype=np.float32), 1), name="c_lt8")
        self.c_iotap = it(np.arange(P, dtype=np.float32).reshape(P, 1), name="c_iotap")
        self.c_iota8 = it(np.arange(8, dtype=np.float32).reshape(8, 1), name="c_iota8")
        self.c_iotaF8 = it(np.broadcast_to(np.arange(8, dtype=np.float32)[None, :], (8, 8)).copy(), name="c_iotaF8")
        self.c_riota = it((np.arange(TT)[None, :] * P + np.arange(P)[:, None]).astype(np.float32), name="c_riota")
        vind = np.zeros((1, VW), dtype=np.float32)
        vind[0, 128::130] = 1.0
        self.c_vind = it(vind, name="c_vind")
        self.c_onesPP = it(np.ones((P, P), dtype=np.float32), name="c_onesPP")
        self.c_i8m999 = it(np.broadcast_to((np.arange(32) % 8 - 999.0).astype(np.float32)[None, :], (P, 32)).copy(), name="c_i8m999")
        self.c_ones8 = it(np.ones((8, 1), dtype=np.float32), name="c_ones8")

    def alloc_sbuf(self):
        nc = self.nc
        BASE = 16512
        A = lambda name, shape, dtype, off: nc.alloc_sbuf_tensor_at(name, shape, dtype, offset=BASE + off)
        self.w1_sb = A("w1_sb", [P, FC, HFF], BF16, 0)
        self.ctx_sb = A("ctx_sb", [P, FC, TT * P], BF16, 0)
        B0 = 65536
        self.w2_sb = A("w2_sb", [P, HC, DM], BF16, B0)
        self.wiaT_sb = A("wiaT_sb", [P, FC, WIA_W], BF16, B0)
        self.kpair_sb = A("kpair_sb", [P, TT * P], BF16, B0 + 49408)
        C0 = 131072
        self.dispfm_sb = A("dispfm_sb", [P, 2, FC, 512], BF16, C0)
        self.h_sb = A("h_sb", [P, HC, 512], BF16, C0 + 16384)
        self.erow = A("erow", [1, TT * P], BF16, C0 + 16384)
        self.lnin_sb = A("lnin_sb", [P, 2, 4, DM], BF16, C0)
        self.gfm_sb = A("gfm_sb", [P, FC, 512], BF16, C0 + 16384)
        self.retst_sb = A("retst_sb", [P, 2, DM], BF16, C0 + 49152)
        self.xln_sb = A("xln_sb", [P, TT, DM], BF16, 0)
        self.vaug_sb = A("vaug_sb", [P, TT, 130], BF16, C0)
        self.E_sb = A("E_sb", [P, 4, 2, 512], BF16, C0 + 8320)
        self.mtmp_sb = A("mtmp_sb", [P, 512], F32, C0 + 12416)
        self.rrow_sb = A("rrow_sb", [P, 512], F32, C0 + 14464)

        self.qpair_sb = A("qpair_sb", [P, TT * P], BF16, C0 + 16512)
        self.xfm_sb = A("xfm_sb", [P, 2, FC, 512], BF16, C0 + 16512)
        self.vev_sb = A("vev_sb", [P, 2, VW], BF16, C0 + 45056)
        self.qkev_sb = A("qkev_sb", [P, 2, 512], BF16, C0 + 49408)
        self.wo_sb = A("wo_sb", [P, 2, FC, 512], BF16, C0)
        self.prj_sb = A("prj_sb", [P, 2, DM], BF16, C0 + 16384)
        self.pout_sb = A("pout_sb", [P, 2, DM], BF16, C0 + 20480)
        D0 = 184320
        off = D0
        def nd(name, shape, dtype):
            nonlocal off
            t = A(name, shape, dtype, off)
            esz = {BF16: 2, F32: 4, I32: 4, U32: 4}[dtype]
            nbytes = int(np.prod(shape[1:])) * esz
            off += (nbytes + 31) // 32 * 32
            return t
        self.ident_f = nd("ident_f", [P, P], F32)
        self.ident_b = nd("ident_b", [P, P], BF16)
        self.stair_b = nd("stair_b", [P, 1152], BF16)
        self.lt8 = nd("lt8", [8, 8], F32)
        self.iotap = nd("iotap", [P, 1], F32)
        self.iota8 = nd("iota8", [8, 1], F32)
        self.iotaF8 = nd("iotaF8", [8, 8], F32)
        self.riota = nd("riota", [P, TT], F32)
        self.vind_b = nd("vind_b", [1, VW], BF16)
        self.onesPP_f = nd("onesPP_f", [P, P], F32)
        self.onesPP_b = nd("onesPP_b", [P, P], BF16)
        self.ones8_f = nd("ones8_f", [8, 1], F32)
        self.pid8 = nd("pid8", [8, 1], F32)
        self.b1_sb = nd("b1_sb", [P, HC], F32)
        self.b2_sb = nd("b2_sb", [1, DM], BF16)
        self.wg_sb = nd("wg_sb", [P, FC, NEXP], BF16)
        self.zero_bf = nd("zero_bf", [P, 512], BF16)
        self.eps_sb = nd("eps_sb", [P, 1], F32)
        self.e_tm = nd("e_tm", [P, TT], F32)
        self.gate_tm = nd("gate_tm", [P, TT], F32)
        self.gfin_tm = nd("gfin_tm", [P, TT], F32)
        self.pos_tm = nd("pos_tm", [P, TT], F32)
        self.dst_i = nd("dst_i", [P, TT], I32)
        self.ridx_i = nd("ridx_i", [P, TT], I32)
        self.didx_i = nd("didx_i", [P, TT], I32)
        self.scr_tm = nd("scr_tm", [P, TT], F32)
        self.msk_tm = nd("msk_tm", [P, TT], F32)
        self.oh_em = nd("oh_em", [8, 512], F32)
        self.loc_em = nd("loc_em", [8, 512], F32)
        self.tmp_em = nd("tmp_em", [8, 512], F32)
        self.carry = nd("carry", [8, 1], F32)
        self.cnt_me = nd("cnt_me", [8, 1], F32)
        self.cnt_all = nd("cnt_all", [8, 8], F32)
        self.ls_all = nd("ls_all", [8, 8], F32)
        self.off_sb = nd("off_sb", [8, 8], F32)
        self.maskE = nd("maskE", [8, 8], F32)
        self.ohr = nd("ohr", [8, 1], F32)
        self.off_em = nd("off_em", [8, 1], F32)
        self.ls_em = nd("ls_em", [8, 1], F32)
        self.om1 = nd("om1", [8, 1], F32)
        self.lm1 = nd("lm1", [8, 1], F32)
        self.offcol = nd("offcol", [8, 1], F32)
        self.lscol = nd("lscol", [8, 1], F32)
        self.delta = nd("delta", [8, 1], F32)
        self.e8s = nd("e8s", [8, 8], F32)
        self.drow = nd("drow", [1, 16], F32)
        self.ddiff = nd("ddiff", [1, 8], F32)
        self.bc_sb = nd("bc_sb", [P, 16], F32)
        self.mv_sb = nd("mv_sb", [P, 2, 2], F32)
        self.stats_sb = nd("stats_sb", [P, 2, 12], F32)
        self.rstd_sb = nd("rstd_sb", [P, 2, 2], F32)
        self.g32_sb = nd("g32_sb", [P, 32], F32)
        self.gexp_sb = nd("gexp_sb", [P, 32], F32)
        self.geq_sb = nd("geq_sb", [P, 32], F32)
        self.gmx_sb = nd("gmx_sb", [P, 8], F32)
        self.gsum_sb = nd("gsum_sb", [P, 4], F32)
        self.grec_sb = nd("grec_sb", [P, 4], F32)
        self.gmax_e = nd("gmax_e", [P, 4], F32)
        self.i8m999 = nd("i8m999", [P, 32], F32)
        self.bias_sb = nd("bias_sb", [P, 2, 2, TT], F32)
        self.etr_sb = nd("etr_sb", [TT, P], BF16)
        self.rrowb_sb = nd("rrowb_sb", [P, 512], BF16)
        self.m64ln2 = nd("m64ln2", [P, 1], F32)
        self.outst_sb = nd("outst_sb", [P, 2, 2, 512], BF16)
        self.xst_sb = nd("xst_sb", [P, 2, DM], BF16)
        self.sbuf_end = off
        assert BASE + off <= 229344, off

    def alloc_psum(self):
        self.ps = [self.nc.alloc_psum_tensor(f"ps{i}", [P, 512], F32) for i in range(8)]

    def p0_load(self):
        self.emit("vector", "v", 1, lambda e: e.memset(self.zero_bf[:], 0.0))
        self.emit("vector", "v", 1, lambda e: e.memset(self.eps_sb[:], EPS))
        self.emit("vector", "v", 1, lambda e: e.memset(self.m64ln2[:], -64.0 * float(np.log(2.0))))
        self.dma("sync", self.ident_f[:], self.c_ident.ap(), "boot")
        self.dma("gpsimd", self.ident_b[:], self.c_ident.ap(), "bootg")
        self.dma("gpsimd", self.stair_b[:], self.c_stair.ap(), "bootg")
        self.dma("sync", self.lt8[:], self.c_lt8.ap(), "boot")
        self.dma("sync", self.iotap[:], self.c_iotap.ap(), "boot")
        self.dma("sync", self.iota8[:], self.c_iota8.ap(), "boot")
        self.dma("sync", self.iotaF8[:], self.c_iotaF8.ap(), "boot")
        self.dma("sync", self.riota[:], self.c_riota.ap(), "boot")
        self.dma("gpsimd", self.vind_b[:], self.c_vind.ap(), "bootg")
        self.dma("sync", self.onesPP_f[:], self.c_onesPP.ap(), "boot")
        self.dma("sync", self.i8m999[:], self.c_i8m999.ap(), "boot")
        self.dma("gpsimd", self.onesPP_b[:], self.c_onesPP.ap(), "bootg")
        self.dma("sync", self.ones8_f[:], self.c_ones8.ap(), "boot")
        self.dma("gpsimd", self.pid8[:], self.nc.partition_id_tensor.ap().broadcast_to((8, 1)), "bootg")
        self.dma("sync", self.b1_sb[:], self.b1_in, "boot")
        self.dma("gpsimd", self.b2_sb[:], self.b2_in, "bootg")
        self.dma("gpsimd", self.wg_sb[:], self.wg_in.rearrange("(k p) e -> p k e", p=P), "bootg")
        self.dma("gpsimd", self.resid, self.x_in.rearrange("(c p) f -> p c f", p=P), "bcr")
        self.dma("gpsimd", self.w1b, self.w1_in, "bcw")
        self.dma("gpsimd", self.w2b, self.w2_in, "bcw")
        self.dma("gpsimd", self.wiaTb, self.wiaT_in, "bca")
        self.dma("gpsimd", self.woTb, self.woT_in, "bca")
        for eng in ENGINES:
            self.wait(eng, "boot", self.cnt["boot"])
            self.wait(eng, "bootg", self.cnt["bootg"])
            self.wait(eng, "bcr", self.cnt["bcr"])

    def emit_ln(self):
        marks = []
        lnin_guard = [0, 0]
        prev_lns = [self.cnt["lns0"], self.cnt["lns1"]]
        for grp in range(8):
            b = grp % 2
            self.wait("sync", "v", lnin_guard[b])
            ld = self.dma("sync", self.lnin_sb[:, b, :, :], self.resid[:, grp * 4:(grp + 1) * 4, :], "lnl%d" % b)
            self.wait("vector", "lnl%d" % b, ld)
            for j in range(4):
                c = grp * 4 + j
                for hh in range(2):
                    self.emit("vector", "v", 1, lambda e, b=b, j=j, hh=hh: e.bn_stats(
                        self.stats_sb[:, j % 2, 6 * hh:6 * hh + 6],
                        self.lnin_sb[:, b, j, 512 * hh:512 * hh + 512]))
                self.wait("vector", "v")
                self.emit("vector", "v", 1, lambda e, j=j: e.bn_aggr(self.mv_sb[:, j % 2, :], self.stats_sb[:, j % 2, :]))
                self.wait("scalar", "v")
                self.emit("scalar", "s", 1, lambda e, j=j: e.activation(self.rstd_sb[:, j % 2, 0:1], self.mv_sb[:, j % 2, 1:2], AF.Ln, bias=self.eps_sb[:]))
                self.wait("scalar", "s")
                self.emit("scalar", "s", 1, lambda e, j=j: e.activation(self.rstd_sb[:, j % 2, 1:2], self.rstd_sb[:, j % 2, 0:1], AF.Exp, scale=-0.5))
                self.wait("vector", "s")
                self.emit("vector", "v", 1, lambda e, b=b, j=j, c=c: e.tensor_scalar(
                    self.xln_sb[:, c, :], self.lnin_sb[:, b, j, :], self.mv_sb[:, j % 2, 0:1], self.rstd_sb[:, j % 2, 1:2],
                    ALU.subtract, ALU.mult))
            lnin_guard[b] = self.cnt["v"]
            self.wait("gpsimd", "v")
            self.wait("gpsimd", "lns%d" % b, prev_lns[b])
            prev_lns[b] = self.dma("gpsimd", self.xln_tm.rearrange("(c p) f -> p c f", p=P)[:, grp * 4:(grp + 1) * 4, :],
                                   self.xln_sb[:, grp * 4:(grp + 1) * 4, :], "lns%d" % b)
            marks.append(("lns%d" % b, prev_lns[b]))
        self.ln_v_mark = self.cnt["v"]
        return marks

    def emit_gating(self, ln_marks):
        t_prev = 0
        gv_mark = 0
        for gch in range(8):
            self.wait("scalar", "t", t_prev)
            sm, mv = ln_marks[gch]
            self.wait("scalar", sm, mv)
            for fc in range(FC):
                self.dmat("scalar", self.gfm_sb[:, fc, :],
                          self.xln_tm[gch * 512:(gch + 1) * 512, fc * P:(fc + 1) * P], "gt")
            self.wait("tensor", "gt")
            self.wait("tensor", "v", gv_mark)
            for m in range(4):
                for k in range(FC):
                    self.emit("tensor", None, make=lambda e, m=m, k=k: e.matmul(
                        self.ps[6][0:P, 8 * m:8 * m + 8], self.gfm_sb[:, k, m * P:(m + 1) * P], self.wg_sb[:, k, :],
                        start=(k == 0), stop=(k == FC - 1)))
            self.mark("tensor", "t")
            t_prev = self.cnt["t"]
            self.wait("vector", "t")
            g32 = self.g32_sb
            gv_mark = self.emit("vector", "v", 1, lambda e: e.tensor_copy(g32[:], self.ps[6][0:P, 0:32]))
            self.wait("vector", "v")
            self.emit("vector", "v", 1, lambda e: e.tensor_reduce(
                self.gmx_sb[:, 0:4], g32[:].rearrange("p (a b) -> p a b", b=8), AX.X, ALU.max))
            self.wait("scalar", "v")
            self.emit("scalar", "s", 1, lambda e: e.activation(self.gexp_sb[:], g32[:], AF.Exp))
            self.wait("vector", "v")
            self.emit("vector", "v", 1, lambda e: e.tensor_tensor(
                self.geq_sb[:].rearrange("p (a b) -> p a b", b=8), g32[:].rearrange("p (a b) -> p a b", b=8),
                self.gmx_sb[:, 0:4].unsqueeze(2).broadcast_to((P, 4, 8)), ALU.is_ge))
            self.wait("vector", "v")
            self.emit("vector", "v", 1, lambda e: e.tensor_tensor(
                self.geq_sb[:], self.geq_sb[:], self.i8m999[:], ALU.mult))
            self.wait("vector", "v")
            self.emit("vector", "v", 1, lambda e: e.tensor_scalar_add(self.geq_sb[:], self.geq_sb[:], 999.0))
            self.wait("vector", "v")
            self.emit("vector", "v", 1, lambda e, gch=gch: e.tensor_reduce(
                self.e_tm[:, gch * 4:gch * 4 + 4],
                self.geq_sb[:].rearrange("p (a b) -> p a b", b=8), AX.X, ALU.min))
            self.wait("vector", "s")
            self.emit("vector", "v", 1, lambda e: e.tensor_reduce(
                self.gsum_sb[:, 0:4], self.gexp_sb[:].rearrange("p (a b) -> p a b", b=8), AX.X, ALU.add))
            self.wait("vector", "v")
            self.emit("vector", "v", 1, lambda e: e.reciprocal(self.grec_sb[:, 0:4], self.gsum_sb[:, 0:4]))
            self.wait("scalar", "v")
            self.emit("scalar", "s", 1, lambda e: e.activation(self.gmax_e[:, 0:4], self.gmx_sb[:, 0:4], AF.Exp))
            self.wait("vector", "s")
            self.wait("vector", "v")
            self.emit("vector", "v", 1, lambda e, gch=gch: e.tensor_tensor(
                self.gate_tm[:, gch * 4:gch * 4 + 4], self.gmax_e[:, 0:4], self.grec_sb[:, 0:4], ALU.mult))
            self.wait("vector", "v")

    def _emit_onehot_scan(self, ch):
        self.wait("tensor", "v")
        self.emit("tensor", None, make=lambda e, ch=ch: e.matmul(
            self.ps[6][0:8, 0:512], self.onesPP_b[0:1, 0:8],
            self.erow[0:1, ch * 512:(ch + 1) * 512], start=True, stop=True))
        self.mark("tensor", "t")
        self.wait("vector", "t")
        self.emit("vector", "v", 1, lambda e: e.tensor_scalar(
            self.oh_em[:], self.ps[6][0:8, 0:512], self.iota8[:], None, ALU.is_equal))
        self.wait("vector", "v")
        self.emit("vector", "v", 1, lambda e: e.tensor_tensor_scan(
            self.loc_em[:], self.oh_em[:], self.oh_em[:], self.carry[:, 0:1], ALU.add, ALU.bypass))
        self.wait("vector", "v")
        self.emit("vector", "v", 1, lambda e: e.tensor_copy(self.carry[:], self.loc_em[:, 511:512]))
        self.wait("vector", "v")

    def emit_routing(self):
        self.wait("tensor", "v")
        self.emit("tensor", "t", 1, lambda e: e.transpose(self.ps[7][0:TT, 0:P], self.e_tm[:], self.ident_f[:]))
        self.wait("vector", "t")
        self.emit("vector", "v", 1, lambda e: e.tensor_copy(self.etr_sb[:], self.ps[7][0:TT, 0:P]))
        self.wait("gpsimd", "v")
        self.dma("gpsimd", self.escr, self.etr_sb[:], "rt")
        self.wait("gpsimd", "rt")
        self.dma("gpsimd", self.erow[:], self.escr.rearrange("t p -> (t p)").unsqueeze(0), "rt")
        self.wait("vector", "rt")

        self.emit("vector", "v", 1, lambda e: e.memset(self.carry[:], 0.0))
        for ch in range(8):
            self._emit_onehot_scan(ch)
        self.emit("vector", "v", 1, lambda e: e.tensor_copy(self.cnt_me[:], self.carry[:]))
        self.wait("gpsimd", "v")
        self.dma("gpsimd", self.agc_in.rearrange("a b -> b a"), self.cnt_me[:], "rt")
        self.wait("gpsimd", "rt")
        self.emit("gpsimd", "cc", 1, lambda e: e.collective_compute(
            "AllGather", ALU.bypass, replica_groups=[list(range(NCORE))],
            ins=[self.agc_in], outs=[self.agc_out]))
        self.wait("gpsimd", "cc")
        cntall_mark = self.dma("gpsimd", self.cnt_all[:], self.agc_out, "rt")

        self.wait("tensor", "v")
        self.emit("tensor", "t", 1, lambda e: e.matmul(self.ps[6][0:8, 9:10], self.lt8[:], self.cnt_me[:], start=True, stop=True))
        self.wait("vector", "t")
        self.emit("vector", "v", 1, lambda e: e.tensor_copy(self.ls_em[:], self.ps[6][0:8, 9:10]))
        self.wait("vector", "v")
        self.emit("vector", "v", 1, lambda e: e.tensor_scalar_add(self.lm1[:], self.ls_em[:], -1.0))

        self.emit("vector", "v", 1, lambda e: e.memset(self.carry[:], 0.0))
        for ch in range(8):
            self._emit_onehot_scan(ch)
            self.emit("vector", "v", 1, lambda e: e.scalar_tensor_tensor(self.tmp_em[:], self.loc_em[:], self.lm1[:], self.oh_em[:], ALU.add, ALU.mult))
            self.wait("tensor", "v")
            for j in range(4):
                t = ch * 4 + j
                self.emit("tensor", None, make=lambda e, j=j, t=t: e.matmul(
                    self.ps[7][0:P, 32 + t:33 + t], self.tmp_em[:, j * 128:(j + 1) * 128], self.ones8_f[:],
                    start=True, stop=True))
            self.mark("tensor", "t")
            self.wait("vector", "t")
            dv = self.emit("vector", "v", 1, lambda e, ch=ch: e.tensor_copy(
                self.dst_i[:, ch * 4:ch * 4 + 4], self.ps[7][0:P, 32 + ch * 4:36 + ch * 4]))
            self.wait("gpsimd", "v", dv)
            for j in range(4):
                c = ch * 4 + j
                self.emit("gpsimd", "ds", 16, lambda e, c=c: e.indirect_dma_start(
                    out=self.compact, out_offset=IndirectOffsetOnAxis(self.dst_i[:, c:c + 1], axis=0),
                    in_=self.xln_sb[:, c, :], in_offset=None))
        self.scat_total = self.cnt["ds"]
        self.wait("gpsimd", "ds")
        self.emit("gpsimd", "cc", 1, lambda e: e.collective_compute(
            "AllGather", ALU.bypass, replica_groups=[list(range(NCORE))],
            ins=[self.compact], outs=[self.ag_x]))

        self.wait("tensor", "rt", cntall_mark)
        self.wait("tensor", "v")
        self.emit("tensor", "t", 1, lambda e: e.matmul(self.ps[6][0:8, 0:8], self.lt8[:], self.cnt_all[:], start=True, stop=True))
        self.wait("vector", "t")
        self.emit("vector", "v", 1, lambda e: e.tensor_copy(self.off_sb[:], self.ps[6][0:8, 0:8]))
        self.emit("vector", "v", 1, lambda e: e.tensor_tensor(self.ohr[:], self.iota8[:], self.pid8[:], ALU.is_equal))
        self.emit("vector", "v", 1, lambda e: e.tensor_scalar(self.maskE[:], self.iotaF8[:], self.pid8[:], None, ALU.is_equal))
        self.wait("vector", "v")
        self.emit("vector", "v", 1, lambda e: e.tensor_scalar(self.e8s[:], self.off_sb[:], self.ohr[:], None, ALU.mult))
        self.wait("tensor", "v")
        self.emit("tensor", "t", 1, lambda e: e.matmul(self.ps[6][0:8, 8:9], self.e8s[:], self.ones8_f[:], start=True, stop=True))
        self.wait("vector", "t")
        self.emit("vector", "v", 1, lambda e: e.tensor_copy(self.off_em[:], self.ps[6][0:8, 8:9]))
        self.wait("vector", "v")
        self.emit("vector", "v", 1, lambda e: e.tensor_scalar_add(self.om1[:], self.off_em[:], -1.0))
        self.emit("vector", "v", 1, lambda e: e.tensor_tensor_scan(self.ls_all[:], self.cnt_all[:], self.cnt_all[:], 0.0, ALU.add, ALU.bypass))
        self.wait("vector", "v")
        self.emit("vector", "v", 1, lambda e: e.tensor_sub(self.ls_all[:], self.ls_all[:], self.cnt_all[:]))
        self.wait("vector", "v")
        self.emit("vector", "v", 1, lambda e: e.tensor_tensor(self.e8s[:], self.off_sb[:], self.maskE[:], ALU.mult))
        self.wait("vector", "v")
        self.emit("vector", "v", 1, lambda e: e.reduce_sum(self.offcol[:], self.e8s[:], axis=AX.X))
        self.wait("vector", "v")
        self.emit("vector", "v", 1, lambda e: e.tensor_tensor(self.e8s[:], self.ls_all[:], self.maskE[:], ALU.mult))
        self.wait("vector", "v")
        self.emit("vector", "v", 1, lambda e: e.reduce_sum(self.lscol[:], self.e8s[:], axis=AX.X))
        self.wait("vector", "v")
        self.emit("vector", "v", 1, lambda e: e.scalar_tensor_tensor(self.delta[:], self.iota8[:], 4096.0, self.lscol[:], ALU.mult, ALU.add))
        self.wait("vector", "v")
        self.emit("vector", "v", 1, lambda e: e.tensor_sub(self.delta[:], self.delta[:], self.offcol[:]))
        self.wait("gpsimd", "v")
        self.dma("gpsimd", self.rowscr[0:8, 0:1], self.delta[:], "rt")
        self.dma("gpsimd", self.rowscr[8:16, 0:1], self.offcol[:], "rt")
        self.wait("gpsimd", "rt")
        self.dma("gpsimd", self.drow[:], self.rowscr.rearrange("a b -> (a b)").unsqueeze(0), "rt")
        self.wait("vector", "rt")
        self.emit("vector", "v", 1, lambda e: e.tensor_sub(self.ddiff[0:1, 1:8], self.drow[0:1, 1:8], self.drow[0:1, 0:7]))
        self.wait("tensor", "v")
        self.emit("tensor", None, make=lambda e: e.matmul(self.ps[7][0:P, 0:1], self.onesPP_f[0:1, :], self.drow[0:1, 0:1], start=True, stop=True))
        for b in range(1, 8):
            self.emit("tensor", None, make=lambda e, b=b: e.matmul(self.ps[7][0:P, b:b + 1], self.onesPP_f[0:1, :], self.ddiff[0:1, b:b + 1], start=True, stop=True))
            self.emit("tensor", None, make=lambda e, b=b: e.matmul(self.ps[7][0:P, 7 + b:8 + b], self.onesPP_f[0:1, :], self.drow[0:1, 8 + b:9 + b], start=True, stop=True))
        self.mark("tensor", "t")
        self.wait("vector", "t")
        self.emit("vector", "v", 1, lambda e: e.tensor_copy(self.bc_sb[:, 0:15], self.ps[7][0:P, 0:15]))
        self.wait("vector", "v")
        self.emit("vector", "v", 1, lambda e: e.tensor_scalar(self.scr_tm[:], self.riota[:], self.bc_sb[:, 0:1], None, ALU.add))
        for b in range(1, 8):
            self.wait("vector", "v")
            self.emit("vector", "v", 1, lambda e, b=b: e.tensor_scalar(self.msk_tm[:], self.riota[:], self.bc_sb[:, 7 + b:8 + b], None, ALU.is_ge))
            self.wait("vector", "v")
            self.emit("vector", "v", 1, lambda e, b=b: e.scalar_tensor_tensor(self.scr_tm[:], self.msk_tm[:], self.bc_sb[:, b:b + 1], self.scr_tm[:], ALU.mult, ALU.add))
        self.wait("vector", "v")
        self.emit("vector", "v", 1, lambda e: e.tensor_scalar_min(self.scr_tm[:], self.scr_tm[:], 32767.0))
        self.wait("vector", "v")
        self.emit("vector", "v", 1, lambda e: e.tensor_copy(self.didx_i[:], self.scr_tm[:]))

        self.emit("vector", "v", 1, lambda e: e.memset(self.carry[:], 0.0))
        for ch in range(8):
            self._emit_onehot_scan(ch)
            self.emit("vector", "v", 1, lambda e: e.scalar_tensor_tensor(self.tmp_em[:], self.loc_em[:], self.om1[:], self.oh_em[:], ALU.add, ALU.mult))
            self.wait("tensor", "v")
            for j in range(4):
                t = ch * 4 + j
                self.emit("tensor", None, make=lambda e, j=j, t=t: e.matmul(
                    self.ps[7][0:P, t:t + 1], self.tmp_em[:, j * 128:(j + 1) * 128], self.ones8_f[:],
                    start=True, stop=True))
            self.mark("tensor", "t")
            self.wait("vector", "t")
        self.emit("vector", "v", 1, lambda e: e.tensor_copy(self.pos_tm[:], self.ps[7][0:P, 0:32]))
        self.wait("vector", "v")
        self.emit("vector", "v", 1, lambda e: e.tensor_scalar(self.msk_tm[:], self.pos_tm[:], 4096.0, None, ALU.is_lt))
        self.wait("vector", "v")
        self.emit("vector", "v", 1, lambda e: e.scalar_tensor_tensor(self.gfin_tm[:], self.gate_tm[:], 0.5, self.msk_tm[:], ALU.mult, ALU.mult))
        self.emit("vector", "v", 1, lambda e: e.scalar_tensor_tensor(self.scr_tm[:], self.e_tm[:], 4096.0, self.pos_tm[:], ALU.mult, ALU.add))
        self.wait("vector", "v")
        self.emit("vector", "v", 1, lambda e: e.tensor_scalar_min(self.scr_tm[:], self.scr_tm[:], 32767.0))
        self.wait("vector", "v")
        self.emit("vector", "v", 1, lambda e: e.tensor_copy(self.ridx_i[:], self.scr_tm[:]))
        self.wait("vector", "v")

    def emit_dispatch(self, ln_marks):
        self.wait("gpsimd", "cc")
        self.wait("gpsimd", "v")
        self.dst_marks = [None] * TT
        dg_marks = [0] * TT
        for c in range(TT):
            b = c % 2
            if c >= 2:
                self.wait("gpsimd", "dst%d" % b, self.dst_marks[c - 2][1])
            dg_marks[c] = self.emit("gpsimd", "dg%d" % b, 16, lambda e, b=b, c=c: e.indirect_dma_start(
                out=self.retst_sb[:, b, :], out_offset=None,
                in_=self.ag_x, in_offset=IndirectOffsetOnAxis(self.didx_i[:, c:c + 1], axis=0)))
            self.wait("sync", "dg%d" % b, dg_marks[c])
            self.dst_marks[c] = ("dst%d" % b, self.dma("sync", self.disp_tm.rearrange("(c p) f -> p c f", p=P)[:, c, :], self.retst_sb[:, b, :], "dst%d" % b))

    def emit_ffn_mm(self):
        t_marks = [0, 0]
        es_marks = [0, 0]
        for grp in range(8):
            gb = grp % 2
            self.wait("scalar", "t", t_marks[gb])
            for cc in range(4):
                sm, mv = self.dst_marks[grp * 4 + cc]
                self.wait("scalar", sm, mv)
            for fc in range(FC):
                self.dmat("scalar", self.dispfm_sb[:, gb, fc, :],
                          self.disp_tm[grp * 512:(grp + 1) * 512, fc * P:(fc + 1) * P], "ft%d" % gb)
            self.wait("tensor", "ft%d" % gb)
            ev_marks = {}
            for hc in range(HC):
                bank = hc % 2
                if hc >= 2:
                    sem, val = ev_marks[hc - 2]
                    self.wait("tensor", sem, val)
                for k in range(FC):
                    self.emit("tensor", None, make=lambda e, hc=hc, k=k, gb=gb, bank=bank: e.matmul(
                        self.ps[bank][0:P, 0:512], self.w1_sb[:, k, hc * P:(hc + 1) * P],
                        self.dispfm_sb[:, gb, k, :],
                        start=(k == 0), stop=(k == FC - 1)))
                self.mark("tensor", "t")
                if hc % 2 == 0:
                    self.wait("scalar", "t")
                    v = self.emit("scalar", "s", 1, lambda e, hc=hc, bank=bank: e.activation(
                        self.h_sb[:, hc, :], self.ps[bank][0:P, 0:512], AF.Relu, bias=self.b1_sb[:, hc:hc + 1]))
                    ev_marks[hc] = ("s", v)
                else:
                    self.wait("vector", "t")
                    v = self.emit("vector", "v", 1, lambda e, hc=hc, bank=bank: e.scalar_tensor_tensor(
                        self.h_sb[:, hc, :], self.ps[bank][0:P, 0:512], self.b1_sb[:, hc:hc + 1],
                        self.zero_bf[:], ALU.add, ALU.max))
                    ev_marks[hc] = ("v", v)
            self.wait("tensor", "s")
            self.wait("tensor", "v")
            for w in range(2):
                if w == 1:
                    self.wait("tensor", "v")
                for m2 in range(2):
                    mslot = w * 2 + m2
                    for n in range(2):
                        bank = 2 + m2 * 2 + n
                        for k2 in range(HC):
                            self.emit("tensor", None, make=lambda e, mslot=mslot, n=n, k2=k2, bank=bank: e.matmul(
                                self.ps[bank][0:P, 0:512], self.h_sb[:, k2, mslot * P:(mslot + 1) * P],
                                self.w2_sb[:, k2, n * 512:(n + 1) * 512],
                                start=(k2 == 0), stop=False))
                        self.emit("tensor", None, make=lambda e, n=n, bank=bank: e.matmul(
                            self.ps[bank][0:P, 0:512], self.onesPP_b[0:1, :], self.b2_sb[0:1, n * 512:(n + 1) * 512],
                            start=False, stop=True))
                self.mark("tensor", "t")
                self.wait("vector", "t")
                for m2 in range(2):
                    r0 = grp * 512 + (w * 2 + m2) * P
                    self.wait("vector", "es%d" % m2, es_marks[m2])
                    for n in range(2):
                        bank = 2 + m2 * 2 + n
                        self.emit("vector", "v", 1, lambda e, m2=m2, n=n, bank=bank: e.tensor_copy(
                            self.outst_sb[:, m2, n, :], self.ps[bank][0:P, 0:512]))
                    self.wait("sync", "v")
                    es_marks[m2] = self.dma("sync", self.expout[r0:r0 + P, :], self.outst_sb[:, m2, :, :].rearrange("p a b -> p (a b)"), "es%d" % m2)
            t_marks[gb] = self.cnt["t"]

    def emit_ffn_return(self, final=False):
        self.wait("gpsimd", "es0")
        self.wait("gpsimd", "es1")
        self.emit("gpsimd", "cc", 1, lambda e: e.collective_compute(
            "AllGather", ALU.bypass, replica_groups=[list(range(NCORE))],
            ins=[self.expout], outs=[self.ag_ret]))
        self.wait("gpsimd", "cc")
        v_marks = [0] * TT
        st_marks = [0] * TT
        stch = "ys" if final else "rs"
        for c in range(TT):
            b = c % 2
            if c >= 2:
                self.wait("gpsimd", "v", v_marks[c - 2])
                self.wait("sync", stch + str(b), st_marks[c - 2])
            g = self.emit("gpsimd", "rg%d" % b, 16, lambda e, b=b, c=c: e.indirect_dma_start(
                out=self.retst_sb[:, b, :], out_offset=None,
                in_=self.ag_ret, in_offset=IndirectOffsetOnAxis(self.ridx_i[:, c:c + 1], axis=0)))
            ld = self.dma("sync", self.xst_sb[:, b, :], self.resid[:, c, :], "rl%d" % b)
            self.wait("vector", "rg%d" % b, g)
            self.wait("vector", "rl%d" % b, ld)
            v_marks[c] = self.emit("vector", "v", 1, lambda e, b=b, c=c: e.scalar_tensor_tensor(
                self.xst_sb[:, b, :], self.retst_sb[:, b, :], self.gfin_tm[:, c:c + 1],
                self.xst_sb[:, b, :], ALU.mult, ALU.add))
            self.wait("gpsimd" if final else "sync", "v", v_marks[c])
            if final:
                st_marks[c] = self.dma("gpsimd", self.y_out.rearrange("(c p) f -> p c f", p=P)[:, c, :], self.xst_sb[:, b, :], "ys%d" % b)
            else:
                st_marks[c] = self.dma("sync", self.resid[:, c, :], self.xst_sb[:, b, :], "rs%d" % b)

    def emit_ffn(self, final=False):
        ln_marks = self.emit_ln()
        self.emit_gating(ln_marks)
        self.emit_routing()
        self.emit_dispatch(ln_marks)
        self.wait("sync", "ds", self.scat_total)
        self.wait("sync", "bcw", self.cnt["bcw"])
        w1l = self.dma("sync", self.w1_sb[:], self.w1b.rearrange("(k p) h -> p k h", p=P), "wf")
        w2l = self.dma("sync", self.w2_sb[:], self.w2b.rearrange("(k p) m -> p k m", p=P), "wf")
        self.wait("tensor", "wf", w2l)
        self.emit_ffn_mm()
        self.emit_ffn_return(final=final)
        self.barrier()

    def emit_qkv(self, ln_marks):
        self.wait("sync", "bca", self.cnt["bca"])
        wl = self.dma("sync", self.wiaT_sb[:], self.wiaTb.rearrange("(k p) f -> p k f", p=P), "wq")
        self.wait("tensor", "wq", wl)
        t_marks = [0, 0]
        s_marks = {}
        qs_marks = {}
        vev_marks = {}
        vq_marks = {}
        for g in range(8):
            gb = g % 2
            self.wait("scalar", "t", t_marks[gb])
            sm, mv = ln_marks[g]
            self.wait("scalar", sm, mv)
            for fc in range(FC):
                self.dmat("scalar", self.xfm_sb[:, gb, fc, :],
                          self.xln_tm[g * 512:(g + 1) * 512, fc * P:(fc + 1) * P], "xt%d" % gb)
            self.wait("tensor", "xt%d" % gb)
            self.wait("tensor", "v", vev_marks.get(g * 4 - 1, 0))
            for m in range(16):
                gm = g * 16 + m
                bank = m % 4
                self.wait("tensor", "s", s_marks.get(gm - 4, 0))
                for k in range(FC):
                    self.emit("tensor", None, make=lambda e, m=m, k=k, gb=gb, bank=bank: e.matmul(
                        self.ps[bank][0:P, 0:512], self.wiaT_sb[:, k, m * P:(m + 1) * P],
                        self.xfm_sb[:, gb, k, :], start=(k == 0), stop=(k == FC - 1)))
                self.mark("tensor", "t")
                self.wait("scalar", "t")
                eb = m % 2
                self.wait("scalar", "qk%d" % eb, qs_marks.get(gm - 2, 0))
                s_marks[gm] = self.emit("scalar", "s", 1, lambda e, bank=bank, eb=eb: e.copy(self.qkev_sb[:, eb, :], self.ps[bank][0:P, 0:512]))
                self.wait("sync", "s")
                qs_marks[gm] = self.dma("sync", self.qk_fm[m * P:(m + 1) * P, g * 512:(g + 1) * 512], self.qkev_sb[:, eb, :], "qk%d" % eb)
            self.wait("tensor", "s")
            for m in range(4):
                c = g * 4 + m
                gv = g * 4 + m
                vb = m % 2
                bank = 4 + vb
                self.wait("tensor", "v", vev_marks.get(gv - 2, 0))
                self.wait("vector", "vst%d" % vb, vq_marks.get(gv - 2, 0))
                for k in range(FC):
                    self.emit("tensor", None, make=lambda e, m=m, k=k, gb=gb, bank=bank: e.matmul(
                        self.ps[bank][0:P, 0:512], self.xfm_sb[:, gb, k, m * P:(m + 1) * P],
                        self.wiaT_sb[:, k, 2048:2560], start=(k == 0), stop=False))
                self.emit("tensor", None, make=lambda e, bank=bank: e.matmul(
                    self.ps[bank][0:P, 0:512], self.onesPP_b[0:1, :], self.vind_b[0:1, 0:512], start=False, stop=True))
                for k in range(FC):
                    self.emit("tensor", None, make=lambda e, m=m, k=k, gb=gb, bank=bank: e.matmul(
                        self.ps[6 if bank == 4 else 7][0:P, 0:512], self.xfm_sb[:, gb, k, m * P:(m + 1) * P],
                        self.wiaT_sb[:, k, 2560:3072], start=(k == 0), stop=False))
                self.emit("tensor", None, make=lambda e, bank=bank: e.matmul(
                    self.ps[6 if bank == 4 else 7][0:P, 0:512], self.onesPP_b[0:1, :], self.vind_b[0:1, 512:1024], start=False, stop=True))
                for k in range(FC):
                    self.emit("tensor", None, make=lambda e, m=m, k=k, gb=gb, vb=vb: e.matmul(
                        self.ps[2 + vb][0:P, 0:16], self.xfm_sb[:, gb, k, m * P:(m + 1) * P],
                        self.wiaT_sb[:, k, 3072:3088], start=(k == 0), stop=False))
                self.emit("tensor", None, make=lambda e, vb=vb: e.matmul(
                    self.ps[2 + vb][0:P, 0:16], self.onesPP_b[0:1, :], self.vind_b[0:1, 1024:1040], start=False, stop=True))
                self.mark("tensor", "t")
                self.wait("vector", "t")
                self.emit("vector", "v", 1, lambda e, vb=vb, bank=bank: e.tensor_copy(self.vev_sb[:, vb, 0:512], self.ps[bank][0:P, 0:512]))
                self.emit("vector", "v", 1, lambda e, vb=vb, bank=bank: e.tensor_copy(self.vev_sb[:, vb, 512:1024], self.ps[6 if bank == 4 else 7][0:P, 0:512]))
                self.emit("vector", "v", 1, lambda e, vb=vb: e.tensor_copy(self.vev_sb[:, vb, 1024:1040], self.ps[2 + vb][0:P, 0:16]))
                vev_marks[gv] = self.cnt["v"]
                self.wait("sync", "v")
                vq_marks[gv] = self.dma("sync", self.vaug[:, c, :], self.vev_sb[:, vb, :], "vst%d" % vb)
            t_marks[gb] = self.cnt["t"]

    def emit_pairs(self):
        for ch in ("qk0", "qk1", "vst0", "vst1"):
            self.wait("sync", ch, self.cnt[ch])
        t_pair = 0
        self.pending_norm = None
        self.norm_marks = (0, 0)
        self.bc_t_mark = 0
        exp_chunk_mark = [0, 0]
        for pr in range(8):
            NQ = NQ_PAIR[pr]
            sA, sB = SLOPE[2 * pr], SLOPE[2 * pr + 1]
            self.wait("sync", "t", t_pair)
            kl = self.dma("sync", self.kpair_sb[:], self.qk_fm[1024 + pr * 128:1024 + (pr + 1) * 128, :], "pl")
            ql = self.dma("sync", self.qpair_sb[:], self.qk_fm[pr * 128:(pr + 1) * 128, :], "pl")
            vl = self.dma("sync", self.vaug_sb[:], self.vaug[:, :, pr * 130:(pr + 1) * 130], "pl")
            self.wait("tensor", "pl", vl)
            self.wait("scalar", "pl", vl)
            for qi in range(4096 // NQ):
                q0 = qi * NQ
                bp = qi % 2
                khi = (q0 + NQ - 1) // 128
                klo = khi
                while klo > 0 and sB * (q0 - ((klo - 1) * 128 + 127)) <= WIN_T:
                    klo -= 1
                nk = khi - klo + 1
                self.wait("vector", "s", exp_chunk_mark[bp])
                self.emit("vector", "v", 1, lambda e, nk=nk, klo=klo, sA=sA, q0=q0, bp=bp: e.tensor_scalar(
                    self.bias_sb[:, bp, 0, 0:nk], self.riota[:, klo:klo + nk], sA, -sA * q0, ALU.mult, ALU.add))
                bias_mark = self.emit("vector", "v", 1, lambda e, nk=nk, klo=klo, sB=sB, q0=q0, bp=bp: e.tensor_scalar(
                    self.bias_sb[:, bp, 1, 0:nk], self.riota[:, klo:klo + nk], sB, -sB * q0, ALU.mult, ALU.add))
                self.wait("scalar", "v", bias_mark)
                exp_marks = {}
                ctx_marks = {}
                sc_marks = {}

                def emit_scores(idx):
                    ki = klo + idx
                    sb = idx % 2
                    diag = ki * 128 + 127 > q0
                    if idx >= 2:
                        self.wait("tensor", "s", exp_marks[idx - 2])
                    self.emit("tensor", None, make=lambda e, ki=ki, sb=sb, diag=diag, q0=q0, NQ=NQ: e.matmul(
                        self.ps[sb][0:P, 0:NQ], self.kpair_sb[0:64, ki * 128:(ki + 1) * 128],
                        self.qpair_sb[0:64, q0:q0 + NQ], start=True, stop=not diag))
                    self.emit("tensor", None, make=lambda e, ki=ki, sb=sb, diag=diag, q0=q0, NQ=NQ: e.matmul(
                        self.ps[2 + sb][0:P, 0:NQ], self.kpair_sb[64:128, ki * 128:(ki + 1) * 128],
                        self.qpair_sb[64:128, q0:q0 + NQ], start=True, stop=not diag))
                    if diag:
                        soffs = 512 + q0 - ki * 128
                        self.emit("tensor", None, make=lambda e, soffs=soffs, sb=sb, NQ=NQ: e.matmul(
                            self.ps[sb][0:P, 0:NQ], self.ident_b[:], self.stair_b[:, soffs:soffs + NQ],
                            start=False, stop=True))
                        self.emit("tensor", None, make=lambda e, soffs=soffs, sb=sb, NQ=NQ: e.matmul(
                            self.ps[2 + sb][0:P, 0:NQ], self.ident_b[:], self.stair_b[:, soffs:soffs + NQ],
                            start=False, stop=True))
                    self.mark("tensor", "t")
                    sc_marks[idx] = self.cnt["t"]

                def emit_exp(idx):
                    sb = idx % 2
                    eb = idx % 4
                    self.wait("scalar", "t", sc_marks[idx])
                    if idx >= 4:
                        self.wait("scalar", "t", ctx_marks[idx - 4])
                    self.emit("scalar", "s", 1, lambda e, sb=sb, eb=eb, idx=idx, NQ=NQ, bp=bp: e.activation(
                        self.E_sb[:, eb, 0, 0:NQ], self.ps[sb][0:P, 0:NQ], AF.Exp,
                        bias=self.bias_sb[:, bp, 0, idx:idx + 1], scale=0.125))
                    exp_marks[idx] = self.emit("scalar", "s", 1, lambda e, sb=sb, eb=eb, idx=idx, NQ=NQ, bp=bp: e.activation(
                        self.E_sb[:, eb, 1, 0:NQ], self.ps[2 + sb][0:P, 0:NQ], AF.Exp,
                        bias=self.bias_sb[:, bp, 1, idx:idx + 1], scale=0.125))

                def emit_ctx(idx):
                    ki = klo + idx
                    eb = idx % 4
                    first = idx == 0
                    last = idx == nk - 1
                    if first:
                        self.wait("tensor", "s", self.norm_marks[0])
                        self.wait("tensor", "v", self.norm_marks[1])
                    self.wait("tensor", "s", exp_marks[idx])
                    self.emit("tensor", None, make=lambda e, ki=ki, eb=eb, NQ=NQ, first=first, last=last: e.matmul(
                        self.ps[4][0:64, 0:NQ], self.vaug_sb[:, ki, 0:64], self.E_sb[:, eb, 0, 0:NQ],
                        start=first, stop=last))
                    self.emit("tensor", None, make=lambda e, ki=ki, eb=eb, NQ=NQ, first=first, last=last: e.matmul(
                        self.ps[5][64:128, 0:NQ], self.vaug_sb[:, ki, 64:128], self.E_sb[:, eb, 1, 0:NQ],
                        start=first, stop=last))
                    self.emit("tensor", None, make=lambda e, ki=ki, eb=eb, NQ=NQ, first=first, last=last: e.matmul(
                        self.ps[6][0:1, 0:NQ], self.vaug_sb[:, ki, 128:129], self.E_sb[:, eb, 0, 0:NQ],
                        start=first, stop=last))
                    self.emit("tensor", None, make=lambda e, ki=ki, eb=eb, NQ=NQ, first=first, last=last: e.matmul(
                        self.ps[6][64:65, 0:NQ], self.vaug_sb[:, ki, 128:129], self.E_sb[:, eb, 1, 0:NQ],
                        start=first, stop=last))
                    self.mark("tensor", "t")
                    ctx_marks[idx] = self.cnt["t"]

                emit_scores(0)
                emit_exp(0)
                if nk > 1:
                    emit_scores(1)
                    emit_exp(1)
                if self.pending_norm is not None:
                    self.pending_norm()
                    self.pending_norm = None
                if nk > 1:
                    emit_ctx(0)
                for idx in range(2, nk):
                    emit_scores(idx)
                    emit_exp(idx)
                    emit_ctx(idx - 1)
                if nk == 1:
                    emit_ctx(0)
                else:
                    emit_ctx(nk - 1)
                exp_chunk_mark[bp] = exp_marks[nk - 1]

                self.wait("vector", "t")
                self.wait("vector", "t", self.bc_t_mark)
                self.emit("vector", "v", 1, lambda e, NQ=NQ: e.reciprocal(self.rrow_sb[0:1, 0:NQ], self.ps[6][0:1, 0:NQ]))
                self.emit("vector", "v", 1, lambda e, NQ=NQ: e.reciprocal(self.rrow_sb[64:65, 0:NQ], self.ps[6][64:65, 0:NQ]))
                self.wait("vector", "v")
                self.emit("vector", "v", 1, lambda e, NQ=NQ: e.tensor_copy(self.rrowb_sb[0:1, 0:NQ], self.rrow_sb[0:1, 0:NQ]))
                rb_mark = self.emit("vector", "v", 1, lambda e, NQ=NQ: e.tensor_copy(self.rrowb_sb[64:65, 0:NQ], self.rrow_sb[64:65, 0:NQ]))

                def make_norm(pr=pr, q0=q0, NQ=NQ, rb_mark=rb_mark):
                    def norm():
                        self.wait("tensor", "v", rb_mark)
                        self.emit("tensor", None, make=lambda e: e.matmul(
                            self.ps[7][0:64, 0:NQ], self.onesPP_b[0:1, 0:64], self.rrowb_sb[0:1, 0:NQ], start=True, stop=True))
                        self.emit("tensor", "t", 1, lambda e: e.matmul(
                            self.ps[7][64:128, 0:NQ], self.onesPP_b[64:65, 0:64], self.rrowb_sb[64:65, 0:NQ], start=True, stop=True))
                        self.bc_t_mark = self.cnt["t"]
                        self.wait("vector", "t")
                        self.emit("vector", "v", 1, lambda e: e.tensor_copy(self.mtmp_sb[0:64, 0:NQ], self.ps[4][0:64, 0:NQ]))
                        self.emit("vector", "v", 1, lambda e: e.tensor_copy(self.mtmp_sb[64:128, 0:NQ], self.ps[5][64:128, 0:NQ]))
                        self.wait("vector", "v")
                        self.emit("vector", "v", 1, lambda e: e.tensor_mul(
                            self.ctx_sb[0:64, pr, q0:q0 + NQ], self.mtmp_sb[0:64, 0:NQ], self.ps[7][0:64, 0:NQ]))
                        vv = self.emit("vector", "v", 1, lambda e: e.tensor_mul(
                            self.ctx_sb[64:128, pr, q0:q0 + NQ], self.mtmp_sb[64:128, 0:NQ], self.ps[7][64:128, 0:NQ]))
                        self.norm_marks = (self.cnt["s"], vv)
                    return norm
                self.pending_norm = make_norm()
            t_pair = self.cnt["t"]
        if self.pending_norm is not None:
            self.pending_norm()
            self.pending_norm = None

    def emit_proj(self):
        w0 = self.dma("sync", self.wo_sb[:, 0, :, :], self.woTb.rearrange("(k p) f -> p k f", p=P)[:, :, 0:512], "wol")
        w1 = self.dma("sync", self.wo_sb[:, 1, :, :], self.woTb.rearrange("(k p) f -> p k f", p=P)[:, :, 512:1024], "wol")
        self.wait("tensor", "wol", w1)
        st_marks = [0, 0]
        v_marks = [0, 0]
        for m in range(TT):
            mb = m % 2
            self.wait("sync", "pjs%d" % mb, st_marks[mb])
            self.wait("sync", "v", v_marks[mb])
            ld = self.dma("sync", self.prj_sb[:, mb, :], self.resid[:, m, :], "pjl%d" % mb)
            self.wait("tensor", "v", v_marks[mb])
            for n in range(2):
                for k in range(FC):
                    self.emit("tensor", None, make=lambda e, m=m, n=n, k=k, mb=mb: e.matmul(
                        self.ps[mb * 2 + n][0:P, 0:512], self.ctx_sb[:, k, m * P:(m + 1) * P],
                        self.wo_sb[:, n, k, :], start=(k == 0), stop=(k == FC - 1)))
            self.mark("tensor", "t")
            self.wait("vector", "t")
            self.wait("vector", "pjl%d" % mb, ld)
            for n in range(2):
                self.emit("vector", "v", 1, lambda e, mb=mb, n=n: e.tensor_tensor(
                    self.pout_sb[:, mb, n * 512:(n + 1) * 512], self.ps[mb * 2 + n][0:P, 0:512],
                    self.prj_sb[:, mb, n * 512:(n + 1) * 512], ALU.add))
            v_marks[mb] = self.cnt["v"]
            self.wait("sync", "v", v_marks[mb])
            st_marks[mb] = self.dma("sync", self.resid[:, m, :], self.pout_sb[:, mb, :], "pjs%d" % mb)

    def emit_attn(self):
        ln_marks = self.emit_ln()
        self.emit_qkv(ln_marks)
        self.emit_pairs()
        self.barrier()
        self.emit_proj()
        self.barrier()

    def store_y_from_resid(self):
        st_marks = [0, 0]
        for c in range(TT):
            b = c % 2
            self.wait("sync", "ys%d" % b, st_marks[b])
            ld = self.dma("sync", self.xst_sb[:, b, :], self.resid[:, c, :], "syl%d" % b)
            self.wait("gpsimd", "syl%d" % b, ld)
            st_marks[b] = self.dma("gpsimd", self.y_out.rearrange("(c p) f -> p c f", p=P)[:, c, :], self.xst_sb[:, b, :], "ys%d" % b)

    def finish(self):
        self.barrier()

    def dbg_route(self):
        self.dbg("e_tm", self.e_tm[:], [P, TT])
        self.dbg("gate_tm", self.gate_tm[:], [P, TT])
        self.dbg("pos_tm", self.pos_tm[:], [P, TT])
        self.dbg("gfin_tm", self.gfin_tm[:], [P, TT])
        self.dbg("cnt_all", self.cnt_all[:], [8, 8])
        self.dbg("didx", self.didx_i[:], [P, TT], I32)
        self.dbg("dst", self.dst_i[:], [P, TT], I32)
        self.dbg("ridx", self.ridx_i[:], [P, TT], I32)

    def build(self):
        nc = self.nc
        self.declare()
        self.alloc_sbuf()
        self.alloc_psum()
        with ExitStack() as es:
            for n in self.SEMS:
                self.sems[n] = es.enter_context(nc.semaphore(n))
            self.p0_load()
            st = self.stage
            if st == "full":
                pass
            elif st == "full2":
                pass
            self.emit_ffn(final=False)
            self.emit_attn()
            self.emit_ffn(final=True)
            self.finish()

            ops = self.ops
            with nc.Block() as block:
                @block.sync
                def _(e):
                    for f in ops["sync"]:
                        f(e)

                @block.tensor
                def _(e):
                    for f in ops["tensor"]:
                        f(e)

                @block.vector
                def _(e):
                    for f in ops["vector"]:
                        f(e)

                @block.scalar
                def _(e):
                    for f in ops["scalar"]:
                        f(e)

                @block.gpsimd
                def _(e):
                    for f in ops["gpsimd"]:
                        f(e)
        return nc


def host_inputs(inputs):
    x = np.asarray(inputs["L0_attn_x"])
    w_in = np.asarray(inputs["L0_attn_w_in"])
    w_out = np.asarray(inputs["L0_attn_w_out"])
    wg = np.ascontiguousarray(np.asarray(inputs["L0_ffn_wg"]))
    w1 = np.asarray(inputs["L0_ffn_w1"])
    b1 = np.asarray(inputs["L0_ffn_b1"])
    w2 = np.asarray(inputs["L0_ffn_w2"])
    b2 = np.asarray(inputs["L0_ffn_b2"])
    wiaT = np.zeros((DM, WIA_W), dtype=np.float32)
    wiaT[:, 0:1024] = w_in[0:1024].T
    wiaT[:, 1024:2048] = w_in[1024:2048].T
    for pr in range(8):
        base = 2048 + 130 * pr
        wiaT[:, base:base + 64] = w_in[2048 + 128 * pr: 2048 + 128 * pr + 64].T
        wiaT[:, base + 64:base + 128] = w_in[2048 + 128 * pr + 64: 2048 + 128 * pr + 128].T
    woT = np.ascontiguousarray(w_out.T)
    maps = []
    for i in range(NCORE):
        maps.append({
            "x_my": np.ascontiguousarray(x[i]),
            "w1_my": np.ascontiguousarray(w1[i]),
            "w2_my": np.ascontiguousarray(w2[i]),
            "b1_my": np.ascontiguousarray(b1[i].reshape(HC, P).T),
            "b2_my": np.ascontiguousarray(b2[i].reshape(1, DM)),
            "wg": wg,
            "wiaT": wiaT,
            "woT": woT,
        })
    return maps


def kernel(**inputs):
    maps = host_inputs(inputs)
    m = MK(stage="full")
    nc = m.build()
    from concourse.bass_utils import run_bass_kernel_spmd
    res = run_bass_kernel_spmd(nc, maps, core_ids=list(range(NCORE)))
    return np.stack([np.asarray(res.results[c]["y"]).astype(np.float32) for c in range(NCORE)])


# version 11 (final)
# speedup vs baseline: 1.1143x; 1.1143x over previous
import numpy as np
from contextlib import ExitStack
import concourse.bass as bass
import concourse.mybir as mybir
from concourse.bass import IndirectOffsetOnAxis

F32 = mybir.dt.float32
BF16 = mybir.dt.bfloat16
I32 = mybir.dt.int32
U32 = mybir.dt.uint32
ALU = mybir.AluOpType
AF = mybir.ActivationFunctionType
AX = mybir.AxisListType

P = 128
TT = 32
DM = 1024
FC = 8
NH = 16
HD = 64
HFF = 4096
HC = 32
CAP = 4096
NEXP = 8
NCORE = 8
EPS = 1e-5
VW = 1040
WIA_W = 2048 + VW

NQ_PAIR = [64, 128, 256, 512, 512, 512, 512, 512]
SLOPE = [2.0 ** (-0.5 * (h + 1)) for h in range(NH)]
WIN_T = 45.0

ENGINES = ("sync", "tensor", "vector", "scalar", "gpsimd")


class MK:
    def __init__(self, stage="full"):
        self.stage = stage
        self.nc = bass.Bass("TRN2", target_bir_lowering=False)
        self.ops = {e: [] for e in ENGINES}
        self.SEMS = (["t", "v", "s", "cc", "bar", "st", "boot", "bootg", "bcr", "bcw", "bca",
                      "gt", "rt", "wq", "pl", "wol", "wf", "ds"]
                     + [f"{b}{i}" for b in ("lnl", "lns", "dl", "ds", "dg", "dst", "ft", "es",
                                            "rg", "rl", "rs", "ys", "xt", "qk", "vst", "pjl", "pjs", "syl")
                        for i in (0, 1)])
        self.cnt = {n: 0 for n in self.SEMS}
        self.sems = {}
        self.dbg_outs = []

    def q(self, eng, fn):
        self.ops[eng].append(fn)

    def emit(self, eng, sem=None, by=1, make=None):
        if sem is None:
            self.q(eng, make)
            return None
        self.cnt[sem] += by
        sems = self.sems
        self.q(eng, lambda e, make=make, sem=sem, by=by: make(e).then_inc(sems[sem], by))
        return self.cnt[sem]

    def wait(self, eng, sem, val=None):
        if val is None:
            val = self.cnt[sem]
        if val <= 0:
            return
        sems = self.sems
        self.q(eng, lambda e, sem=sem, val=val: e.wait_ge(sems[sem], val))

    def mark(self, eng, sem, by=1):
        self.cnt[sem] += by
        last = self.ops[eng].pop()
        sems = self.sems
        self.ops[eng].append(lambda e, last=last, sem=sem, by=by: last(e).then_inc(sems[sem], by))
        return self.cnt[sem]

    def dma(self, eng, out, in_, sem):
        return self.emit(eng, sem, 16, lambda e, out=out, in_=in_: e.dma_start(out=out, in_=in_))

    def dmat(self, eng, out, in_, sem):
        return self.emit(eng, sem, 16, lambda e, out=out, in_=in_: e.dma_start_transpose(out, in_))

    def barrier(self):
        for eng in ENGINES:
            for s in self.SEMS:
                if s != "bar" and self.cnt[s] > 0:
                    self.wait(eng, s, self.cnt[s])
            self.emit(eng, "bar", 1, lambda e: e.nop())
        tot = self.cnt["bar"]
        for eng in ENGINES:
            self.wait(eng, "bar", tot)

    def dbg(self, name, ap_src, shape, dtype=F32):
        t = self.nc.dram_tensor("DBG_" + name, list(shape), dtype, kind="ExternalOutput").ap()
        self.dbg_outs.append(name)
        self.dma("sync", t, ap_src, "st")

    def declare(self):
        nc = self.nc
        self.x_in = nc.dram_tensor("x_my", [TT * P, DM], F32, kind="ExternalInput").ap()
        self.w1_in = nc.dram_tensor("w1_my", [DM, HFF], F32, kind="ExternalInput").ap()
        self.w2_in = nc.dram_tensor("w2_my", [HFF, DM], F32, kind="ExternalInput").ap()
        self.b1_in = nc.dram_tensor("b1_my", [P, HC], F32, kind="ExternalInput").ap()
        self.b2_in = nc.dram_tensor("b2_my", [1, DM], F32, kind="ExternalInput").ap()
        self.wg_in = nc.dram_tensor("wg", [DM, NEXP], F32, kind="ExternalInput").ap()
        self.wiaT_in = nc.dram_tensor("wiaT", [DM, WIA_W], F32, kind="ExternalInput").ap()
        self.woT_in = nc.dram_tensor("woT", [DM, DM], F32, kind="ExternalInput").ap()
        self.y_out = nc.dram_tensor("y", [TT * P, DM], F32, kind="ExternalOutput").ap()

        dt = nc.dram_tensor
        self.resid = dt("resid", [P, TT, DM], BF16, kind="Internal").ap()
        self.xln_tm = dt("xln_tm", [TT * P, DM], BF16, kind="Internal").ap()
        self.compact = dt("compact", [CAP, DM], BF16, kind="Internal").ap()
        self.ag_x = dt("ag_x", [NCORE * CAP, DM], BF16, kind="Internal", addr_space="Shared").ap()
        self.disp_tm = dt("disp_tm", [CAP, DM], BF16, kind="Internal").ap()
        self.expout = dt("expout", [CAP, DM], BF16, kind="Internal").ap()
        self.ag_ret = dt("ag_ret", [NCORE * CAP, DM], BF16, kind="Internal", addr_space="Shared").ap()
        self.qk_fm = dt("qk_fm", [2048, TT * P], BF16, kind="Internal").ap()
        self.vaug = dt("vaug", [P, TT, VW], BF16, kind="Internal").ap()
        self.w1b = dt("w1b", [DM, HFF], BF16, kind="Internal").ap()
        self.w2b = dt("w2b", [HFF, DM], BF16, kind="Internal").ap()
        self.wiaTb = dt("wiaTb", [DM, WIA_W], BF16, kind="Internal").ap()
        self.woTb = dt("woTb", [DM, DM], BF16, kind="Internal").ap()
        self.escr = dt("escr", [TT, P], BF16, kind="Internal").ap()
        self.rowscr = dt("rowscr", [16, 1], F32, kind="Internal").ap()
        self.agc_in = dt("agc_in", [1, NEXP], F32, kind="Internal").ap()
        self.agc_out = dt("agc_out", [NCORE, NEXP], F32, kind="Internal", addr_space="Shared").ap()

        it = nc.inline_tensor
        self.c_ident = it(np.eye(P, dtype=np.float32), name="c_ident")
        jj = np.arange(1152)[None, :]
        pp = np.arange(P)[:, None]
        self.c_stair = it(np.where((jj - 512) < pp, -30000.0, 0.0).astype(np.float32), name="c_stair")
        self.c_lt8 = it(np.triu(np.ones((8, 8), dtype=np.float32), 1), name="c_lt8")
        self.c_iotap = it(np.arange(P, dtype=np.float32).reshape(P, 1), name="c_iotap")
        self.c_iota8 = it(np.arange(8, dtype=np.float32).reshape(8, 1), name="c_iota8")
        self.c_iotaF8 = it(np.broadcast_to(np.arange(8, dtype=np.float32)[None, :], (8, 8)).copy(), name="c_iotaF8")
        self.c_riota = it((np.arange(TT)[None, :] * P + np.arange(P)[:, None]).astype(np.float32), name="c_riota")
        vind = np.zeros((1, VW), dtype=np.float32)
        vind[0, 128::130] = 1.0
        self.c_vind = it(vind, name="c_vind")
        self.c_onesPP = it(np.ones((P, P), dtype=np.float32), name="c_onesPP")
        self.c_i8m999 = it(np.broadcast_to((np.arange(32) % 8 - 999.0).astype(np.float32)[None, :], (P, 32)).copy(), name="c_i8m999")
        self.c_ones8 = it(np.ones((8, 1), dtype=np.float32), name="c_ones8")

    def alloc_sbuf(self):
        nc = self.nc
        BASE = 16512
        A = lambda name, shape, dtype, off: nc.alloc_sbuf_tensor_at(name, shape, dtype, offset=BASE + off)
        self.w1_sb = A("w1_sb", [P, FC, HFF], BF16, 0)
        self.ctx_sb = A("ctx_sb", [P, FC, TT * P], BF16, 0)
        B0 = 65536
        self.w2_sb = A("w2_sb", [P, HC, DM], BF16, B0)
        self.wiaT_sb = A("wiaT_sb", [P, FC, WIA_W], BF16, B0)
        self.kpair_sb = A("kpair_sb", [P, TT * P], BF16, B0 + 49408)
        C0 = 131072
        self.dispfm_sb = A("dispfm_sb", [P, 2, FC, 512], BF16, C0)
        self.h_sb = A("h_sb", [P, HC, 512], BF16, C0 + 16384)
        self.erow = A("erow", [1, TT * P], BF16, C0 + 16384)
        self.lnin_sb = A("lnin_sb", [P, 2, 4, DM], BF16, C0)
        self.gfm_sb = A("gfm_sb", [P, FC, 512], BF16, C0 + 16384)
        self.retst_sb = A("retst_sb", [P, 2, DM], BF16, C0 + 49152)
        self.xln_sb = A("xln_sb", [P, TT, DM], BF16, 0)
        self.vaug_sb = A("vaug_sb", [P, TT, 130], BF16, C0)
        self.E_sb = A("E_sb", [P, 4, 2, 512], BF16, C0 + 8320)
        self.mtmp_sb = A("mtmp_sb", [P, 512], F32, C0 + 12416)
        self.rrow_sb = A("rrow_sb", [P, 512], F32, C0 + 14464)

        self.qpair_sb = A("qpair_sb", [P, TT * P], BF16, C0 + 16512)
        self.xfm_sb = A("xfm_sb", [P, 2, FC, 512], BF16, C0 + 16512)
        self.vev_sb = A("vev_sb", [P, 2, VW], BF16, C0 + 45056)
        self.qkev_sb = A("qkev_sb", [P, 2, 512], BF16, C0 + 49408)
        self.wo_sb = A("wo_sb", [P, 2, FC, 512], BF16, C0)
        self.prj_sb = A("prj_sb", [P, 2, DM], BF16, C0 + 16384)
        self.pout_sb = A("pout_sb", [P, 2, DM], BF16, C0 + 20480)
        D0 = 184320
        off = D0
        def nd(name, shape, dtype):
            nonlocal off
            t = A(name, shape, dtype, off)
            esz = {BF16: 2, F32: 4, I32: 4, U32: 4}[dtype]
            nbytes = int(np.prod(shape[1:])) * esz
            off += (nbytes + 31) // 32 * 32
            return t
        self.ident_f = nd("ident_f", [P, P], F32)
        self.ident_b = nd("ident_b", [P, P], BF16)
        self.stair_b = nd("stair_b", [P, 1152], BF16)
        self.lt8 = nd("lt8", [8, 8], F32)
        self.iotap = nd("iotap", [P, 1], F32)
        self.iota8 = nd("iota8", [8, 1], F32)
        self.iotaF8 = nd("iotaF8", [8, 8], F32)
        self.riota = nd("riota", [P, TT], F32)
        self.vind_b = nd("vind_b", [1, VW], BF16)
        self.onesPP_f = nd("onesPP_f", [P, P], F32)
        self.onesPP_b = nd("onesPP_b", [P, P], BF16)
        self.ones8_f = nd("ones8_f", [8, 1], F32)
        self.pid8 = nd("pid8", [8, 1], F32)
        self.b1_sb = nd("b1_sb", [P, HC], F32)
        self.b2_sb = nd("b2_sb", [1, DM], BF16)
        self.wg_sb = nd("wg_sb", [P, FC, NEXP], BF16)
        self.zero_bf = nd("zero_bf", [P, 512], BF16)
        self.eps_sb = nd("eps_sb", [P, 1], F32)
        self.e_tm = nd("e_tm", [P, TT], F32)
        self.gate_tm = nd("gate_tm", [P, TT], F32)
        self.gfin_tm = nd("gfin_tm", [P, TT], F32)
        self.pos_tm = nd("pos_tm", [P, TT], F32)
        self.dst_i = nd("dst_i", [P, TT], I32)
        self.ridx_i = nd("ridx_i", [P, TT], I32)
        self.didx_i = nd("didx_i", [P, TT], I32)
        self.scr_tm = nd("scr_tm", [P, TT], F32)
        self.msk_tm = nd("msk_tm", [P, TT], F32)
        self.oh_em = nd("oh_em", [8, 512], F32)
        self.loc_em = nd("loc_em", [8, 512], F32)
        self.tmp_em = nd("tmp_em", [8, 512], F32)
        self.carry = nd("carry", [8, 1], F32)
        self.cnt_me = nd("cnt_me", [8, 1], F32)
        self.cnt_all = nd("cnt_all", [8, 8], F32)
        self.ls_all = nd("ls_all", [8, 8], F32)
        self.off_sb = nd("off_sb", [8, 8], F32)
        self.maskE = nd("maskE", [8, 8], F32)
        self.ohr = nd("ohr", [8, 1], F32)
        self.off_em = nd("off_em", [8, 1], F32)
        self.ls_em = nd("ls_em", [8, 1], F32)
        self.om1 = nd("om1", [8, 1], F32)
        self.lm1 = nd("lm1", [8, 1], F32)
        self.offcol = nd("offcol", [8, 1], F32)
        self.lscol = nd("lscol", [8, 1], F32)
        self.delta = nd("delta", [8, 1], F32)
        self.e8s = nd("e8s", [8, 8], F32)
        self.drow = nd("drow", [1, 16], F32)
        self.ddiff = nd("ddiff", [1, 8], F32)
        self.bc_sb = nd("bc_sb", [P, 16], F32)
        self.mv_sb = nd("mv_sb", [P, 2, 2], F32)
        self.stats_sb = nd("stats_sb", [P, 2, 12], F32)
        self.rstd_sb = nd("rstd_sb", [P, 2, 2], F32)
        self.g32_sb = nd("g32_sb", [P, 32], F32)
        self.gexp_sb = nd("gexp_sb", [P, 32], F32)
        self.geq_sb = nd("geq_sb", [P, 32], F32)
        self.gmx_sb = nd("gmx_sb", [P, 8], F32)
        self.gsum_sb = nd("gsum_sb", [P, 4], F32)
        self.grec_sb = nd("grec_sb", [P, 4], F32)
        self.gmax_e = nd("gmax_e", [P, 4], F32)
        self.i8m999 = nd("i8m999", [P, 32], F32)
        self.bias_sb = nd("bias_sb", [P, 2, 2, TT], F32)
        self.etr_sb = nd("etr_sb", [TT, P], BF16)
        self.rrowb_sb = nd("rrowb_sb", [P, 512], BF16)
        self.m64ln2 = nd("m64ln2", [P, 1], F32)
        self.outst_sb = nd("outst_sb", [P, 2, 2, 512], BF16)
        self.xst_sb = nd("xst_sb", [P, 2, DM], BF16)
        self.sbuf_end = off
        assert BASE + off <= 229344, off

    def alloc_psum(self):
        self.ps = [self.nc.alloc_psum_tensor(f"ps{i}", [P, 512], F32) for i in range(8)]

    def p0_load(self):
        self.emit("vector", "v", 1, lambda e: e.memset(self.zero_bf[:], 0.0))
        self.emit("vector", "v", 1, lambda e: e.memset(self.eps_sb[:], EPS))
        self.emit("vector", "v", 1, lambda e: e.memset(self.m64ln2[:], -64.0 * float(np.log(2.0))))
        self.dma("sync", self.ident_f[:], self.c_ident.ap(), "boot")
        self.dma("gpsimd", self.ident_b[:], self.c_ident.ap(), "bootg")
        self.dma("gpsimd", self.stair_b[:], self.c_stair.ap(), "bootg")
        self.dma("sync", self.lt8[:], self.c_lt8.ap(), "boot")
        self.dma("sync", self.iotap[:], self.c_iotap.ap(), "boot")
        self.dma("sync", self.iota8[:], self.c_iota8.ap(), "boot")
        self.dma("sync", self.iotaF8[:], self.c_iotaF8.ap(), "boot")
        self.dma("sync", self.riota[:], self.c_riota.ap(), "boot")
        self.dma("gpsimd", self.vind_b[:], self.c_vind.ap(), "bootg")
        self.dma("sync", self.onesPP_f[:], self.c_onesPP.ap(), "boot")
        self.dma("sync", self.i8m999[:], self.c_i8m999.ap(), "boot")
        self.dma("gpsimd", self.onesPP_b[:], self.c_onesPP.ap(), "bootg")
        self.dma("sync", self.ones8_f[:], self.c_ones8.ap(), "boot")
        self.dma("gpsimd", self.pid8[:], self.nc.partition_id_tensor.ap().broadcast_to((8, 1)), "bootg")
        self.dma("sync", self.b1_sb[:], self.b1_in, "boot")
        self.dma("gpsimd", self.b2_sb[:], self.b2_in, "bootg")
        self.dma("gpsimd", self.wg_sb[:], self.wg_in.rearrange("(k p) e -> p k e", p=P), "bootg")
        self.dma("gpsimd", self.resid, self.x_in.rearrange("(c p) f -> p c f", p=P), "bcr")
        self.dma("gpsimd", self.w1b, self.w1_in, "bcw")
        self.dma("gpsimd", self.w2b, self.w2_in, "bcw")
        self.dma("gpsimd", self.wiaTb, self.wiaT_in, "bca")
        self.dma("gpsimd", self.woTb, self.woT_in, "bca")
        for eng in ENGINES:
            self.wait(eng, "boot", self.cnt["boot"])
            self.wait(eng, "bootg", self.cnt["bootg"])
            self.wait(eng, "bcr", self.cnt["bcr"])

    def emit_ln(self):
        marks = []
        lnin_guard = [0, 0]
        prev_lns = [self.cnt["lns0"], self.cnt["lns1"]]
        for grp in range(8):
            b = grp % 2
            self.wait("sync", "v", lnin_guard[b])
            ld = self.dma("sync", self.lnin_sb[:, b, :, :], self.resid[:, grp * 4:(grp + 1) * 4, :], "lnl%d" % b)
            self.wait("vector", "lnl%d" % b, ld)
            for j in range(4):
                c = grp * 4 + j
                for hh in range(2):
                    self.emit("vector", "v", 1, lambda e, b=b, j=j, hh=hh: e.bn_stats(
                        self.stats_sb[:, j % 2, 6 * hh:6 * hh + 6],
                        self.lnin_sb[:, b, j, 512 * hh:512 * hh + 512]))
                self.wait("vector", "v")
                self.emit("vector", "v", 1, lambda e, j=j: e.bn_aggr(self.mv_sb[:, j % 2, :], self.stats_sb[:, j % 2, :]))
                self.wait("scalar", "v")
                self.emit("scalar", "s", 1, lambda e, j=j: e.activation(self.rstd_sb[:, j % 2, 0:1], self.mv_sb[:, j % 2, 1:2], AF.Ln, bias=self.eps_sb[:]))
                self.wait("scalar", "s")
                self.emit("scalar", "s", 1, lambda e, j=j: e.activation(self.rstd_sb[:, j % 2, 1:2], self.rstd_sb[:, j % 2, 0:1], AF.Exp, scale=-0.5))
                self.wait("vector", "s")
                self.emit("vector", "v", 1, lambda e, b=b, j=j, c=c: e.tensor_scalar(
                    self.xln_sb[:, c, :], self.lnin_sb[:, b, j, :], self.mv_sb[:, j % 2, 0:1], self.rstd_sb[:, j % 2, 1:2],
                    ALU.subtract, ALU.mult))
            lnin_guard[b] = self.cnt["v"]
            self.wait("gpsimd", "v")
            self.wait("gpsimd", "lns%d" % b, prev_lns[b])
            prev_lns[b] = self.dma("gpsimd", self.xln_tm.rearrange("(c p) f -> p c f", p=P)[:, grp * 4:(grp + 1) * 4, :],
                                   self.xln_sb[:, grp * 4:(grp + 1) * 4, :], "lns%d" % b)
            marks.append(("lns%d" % b, prev_lns[b]))
        self.ln_v_mark = self.cnt["v"]
        return marks

    def emit_gating(self, ln_marks):
        t_prev = 0
        gv_mark = 0
        for gch in range(8):
            self.wait("scalar", "t", t_prev)
            sm, mv = ln_marks[gch]
            self.wait("scalar", sm, mv)
            for fc in range(FC):
                self.dmat("scalar", self.gfm_sb[:, fc, :],
                          self.xln_tm[gch * 512:(gch + 1) * 512, fc * P:(fc + 1) * P], "gt")
            self.wait("tensor", "gt")
            self.wait("tensor", "v", gv_mark)
            for m in range(4):
                for k in range(FC):
                    self.emit("tensor", None, make=lambda e, m=m, k=k: e.matmul(
                        self.ps[6][0:P, 8 * m:8 * m + 8], self.gfm_sb[:, k, m * P:(m + 1) * P], self.wg_sb[:, k, :],
                        start=(k == 0), stop=(k == FC - 1)))
            self.mark("tensor", "t")
            t_prev = self.cnt["t"]
            self.wait("vector", "t")
            g32 = self.g32_sb
            gv_mark = self.emit("vector", "v", 1, lambda e: e.tensor_copy(g32[:], self.ps[6][0:P, 0:32]))
            self.wait("vector", "v")
            self.emit("vector", "v", 1, lambda e: e.tensor_reduce(
                self.gmx_sb[:, 0:4], g32[:].rearrange("p (a b) -> p a b", b=8), AX.X, ALU.max))
            self.wait("scalar", "v")
            self.emit("scalar", "s", 1, lambda e: e.activation(self.gexp_sb[:], g32[:], AF.Exp))
            self.wait("vector", "v")
            self.emit("vector", "v", 1, lambda e: e.tensor_tensor(
                self.geq_sb[:].rearrange("p (a b) -> p a b", b=8), g32[:].rearrange("p (a b) -> p a b", b=8),
                self.gmx_sb[:, 0:4].unsqueeze(2).broadcast_to((P, 4, 8)), ALU.is_ge))
            self.wait("vector", "v")
            self.emit("vector", "v", 1, lambda e: e.tensor_tensor(
                self.geq_sb[:], self.geq_sb[:], self.i8m999[:], ALU.mult))
            self.wait("vector", "v")
            self.emit("vector", "v", 1, lambda e: e.tensor_scalar_add(self.geq_sb[:], self.geq_sb[:], 999.0))
            self.wait("vector", "v")
            self.emit("vector", "v", 1, lambda e, gch=gch: e.tensor_reduce(
                self.e_tm[:, gch * 4:gch * 4 + 4],
                self.geq_sb[:].rearrange("p (a b) -> p a b", b=8), AX.X, ALU.min))
            self.wait("vector", "s")
            self.emit("vector", "v", 1, lambda e: e.tensor_reduce(
                self.gsum_sb[:, 0:4], self.gexp_sb[:].rearrange("p (a b) -> p a b", b=8), AX.X, ALU.add))
            self.wait("vector", "v")
            self.emit("vector", "v", 1, lambda e: e.reciprocal(self.grec_sb[:, 0:4], self.gsum_sb[:, 0:4]))
            self.wait("scalar", "v")
            self.emit("scalar", "s", 1, lambda e: e.activation(self.gmax_e[:, 0:4], self.gmx_sb[:, 0:4], AF.Exp))
            self.wait("vector", "s")
            self.wait("vector", "v")
            self.emit("vector", "v", 1, lambda e, gch=gch: e.tensor_tensor(
                self.gate_tm[:, gch * 4:gch * 4 + 4], self.gmax_e[:, 0:4], self.grec_sb[:, 0:4], ALU.mult))
            self.wait("vector", "v")

    def _emit_onehot_scan(self, ch):
        self.wait("tensor", "v")
        self.emit("tensor", None, make=lambda e, ch=ch: e.matmul(
            self.ps[6][0:8, 0:512], self.onesPP_b[0:1, 0:8],
            self.erow[0:1, ch * 512:(ch + 1) * 512], start=True, stop=True))
        self.mark("tensor", "t")
        self.wait("vector", "t")
        self.emit("vector", "v", 1, lambda e: e.tensor_scalar(
            self.oh_em[:], self.ps[6][0:8, 0:512], self.iota8[:], None, ALU.is_equal))
        self.wait("vector", "v")
        self.emit("vector", "v", 1, lambda e: e.tensor_tensor_scan(
            self.loc_em[:], self.oh_em[:], self.oh_em[:], self.carry[:, 0:1], ALU.add, ALU.bypass))
        self.wait("vector", "v")
        self.emit("vector", "v", 1, lambda e: e.tensor_copy(self.carry[:], self.loc_em[:, 511:512]))
        self.wait("vector", "v")

    def emit_routing(self):
        self.wait("tensor", "v")
        self.emit("tensor", "t", 1, lambda e: e.transpose(self.ps[7][0:TT, 0:P], self.e_tm[:], self.ident_f[:]))
        self.wait("vector", "t")
        self.emit("vector", "v", 1, lambda e: e.tensor_copy(self.etr_sb[:], self.ps[7][0:TT, 0:P]))
        self.wait("gpsimd", "v")
        self.dma("gpsimd", self.escr, self.etr_sb[:], "rt")
        self.wait("gpsimd", "rt")
        self.dma("gpsimd", self.erow[:], self.escr.rearrange("t p -> (t p)").unsqueeze(0), "rt")
        self.wait("vector", "rt")

        self.emit("vector", "v", 1, lambda e: e.memset(self.carry[:], 0.0))
        for ch in range(8):
            self._emit_onehot_scan(ch)
        self.emit("vector", "v", 1, lambda e: e.tensor_copy(self.cnt_me[:], self.carry[:]))
        self.wait("gpsimd", "v")
        self.dma("gpsimd", self.agc_in.rearrange("a b -> b a"), self.cnt_me[:], "rt")
        self.wait("gpsimd", "rt")
        self.emit("gpsimd", "cc", 1, lambda e: e.collective_compute(
            "AllGather", ALU.bypass, replica_groups=[list(range(NCORE))],
            ins=[self.agc_in], outs=[self.agc_out]))
        self.wait("gpsimd", "cc")
        cntall_mark = self.dma("gpsimd", self.cnt_all[:], self.agc_out, "rt")

        self.wait("tensor", "v")
        self.emit("tensor", "t", 1, lambda e: e.matmul(self.ps[6][0:8, 9:10], self.lt8[:], self.cnt_me[:], start=True, stop=True))
        self.wait("vector", "t")
        self.emit("vector", "v", 1, lambda e: e.tensor_copy(self.ls_em[:], self.ps[6][0:8, 9:10]))
        self.wait("vector", "v")
        self.emit("vector", "v", 1, lambda e: e.tensor_scalar_add(self.lm1[:], self.ls_em[:], -1.0))

        self.emit("vector", "v", 1, lambda e: e.memset(self.carry[:], 0.0))
        for ch in range(8):
            self._emit_onehot_scan(ch)
            self.emit("vector", "v", 1, lambda e: e.scalar_tensor_tensor(self.tmp_em[:], self.loc_em[:], self.lm1[:], self.oh_em[:], ALU.add, ALU.mult))
            self.wait("tensor", "v")
            for j in range(4):
                t = ch * 4 + j
                self.emit("tensor", None, make=lambda e, j=j, t=t: e.matmul(
                    self.ps[7][0:P, 32 + t:33 + t], self.tmp_em[:, j * 128:(j + 1) * 128], self.ones8_f[:],
                    start=True, stop=True))
            self.mark("tensor", "t")
            self.wait("vector", "t")
            dv = self.emit("vector", "v", 1, lambda e, ch=ch: e.tensor_copy(
                self.dst_i[:, ch * 4:ch * 4 + 4], self.ps[7][0:P, 32 + ch * 4:36 + ch * 4]))
            self.wait("gpsimd", "v", dv)
            for j in range(4):
                c = ch * 4 + j
                self.emit("gpsimd", "ds", 16, lambda e, c=c: e.indirect_dma_start(
                    out=self.compact, out_offset=IndirectOffsetOnAxis(self.dst_i[:, c:c + 1], axis=0),
                    in_=self.xln_sb[:, c, :], in_offset=None))
        self.scat_total = self.cnt["ds"]
        self.wait("gpsimd", "ds")
        self.emit("gpsimd", "cc", 1, lambda e: e.collective_compute(
            "AllGather", ALU.bypass, replica_groups=[list(range(NCORE))],
            ins=[self.compact], outs=[self.ag_x]))

        self.wait("tensor", "rt", cntall_mark)
        self.wait("tensor", "v")
        self.emit("tensor", "t", 1, lambda e: e.matmul(self.ps[6][0:8, 0:8], self.lt8[:], self.cnt_all[:], start=True, stop=True))
        self.wait("vector", "t")
        self.emit("vector", "v", 1, lambda e: e.tensor_copy(self.off_sb[:], self.ps[6][0:8, 0:8]))
        self.emit("vector", "v", 1, lambda e: e.tensor_tensor(self.ohr[:], self.iota8[:], self.pid8[:], ALU.is_equal))
        self.emit("vector", "v", 1, lambda e: e.tensor_scalar(self.maskE[:], self.iotaF8[:], self.pid8[:], None, ALU.is_equal))
        self.wait("vector", "v")
        self.emit("vector", "v", 1, lambda e: e.tensor_scalar(self.e8s[:], self.off_sb[:], self.ohr[:], None, ALU.mult))
        self.wait("tensor", "v")
        self.emit("tensor", "t", 1, lambda e: e.matmul(self.ps[6][0:8, 8:9], self.e8s[:], self.ones8_f[:], start=True, stop=True))
        self.wait("vector", "t")
        self.emit("vector", "v", 1, lambda e: e.tensor_copy(self.off_em[:], self.ps[6][0:8, 8:9]))
        self.wait("vector", "v")
        self.emit("vector", "v", 1, lambda e: e.tensor_scalar_add(self.om1[:], self.off_em[:], -1.0))
        self.emit("vector", "v", 1, lambda e: e.tensor_tensor_scan(self.ls_all[:], self.cnt_all[:], self.cnt_all[:], 0.0, ALU.add, ALU.bypass))
        self.wait("vector", "v")
        self.emit("vector", "v", 1, lambda e: e.tensor_sub(self.ls_all[:], self.ls_all[:], self.cnt_all[:]))
        self.wait("vector", "v")
        self.emit("vector", "v", 1, lambda e: e.tensor_tensor(self.e8s[:], self.off_sb[:], self.maskE[:], ALU.mult))
        self.wait("vector", "v")
        self.emit("vector", "v", 1, lambda e: e.reduce_sum(self.offcol[:], self.e8s[:], axis=AX.X))
        self.wait("vector", "v")
        self.emit("vector", "v", 1, lambda e: e.tensor_tensor(self.e8s[:], self.ls_all[:], self.maskE[:], ALU.mult))
        self.wait("vector", "v")
        self.emit("vector", "v", 1, lambda e: e.reduce_sum(self.lscol[:], self.e8s[:], axis=AX.X))
        self.wait("vector", "v")
        self.emit("vector", "v", 1, lambda e: e.scalar_tensor_tensor(self.delta[:], self.iota8[:], 4096.0, self.lscol[:], ALU.mult, ALU.add))
        self.wait("vector", "v")
        self.emit("vector", "v", 1, lambda e: e.tensor_sub(self.delta[:], self.delta[:], self.offcol[:]))
        self.wait("gpsimd", "v")
        self.dma("gpsimd", self.rowscr[0:8, 0:1], self.delta[:], "rt")
        self.dma("gpsimd", self.rowscr[8:16, 0:1], self.offcol[:], "rt")
        self.wait("gpsimd", "rt")
        self.dma("gpsimd", self.drow[:], self.rowscr.rearrange("a b -> (a b)").unsqueeze(0), "rt")
        self.wait("vector", "rt")
        self.emit("vector", "v", 1, lambda e: e.tensor_sub(self.ddiff[0:1, 1:8], self.drow[0:1, 1:8], self.drow[0:1, 0:7]))
        self.wait("tensor", "v")
        self.emit("tensor", None, make=lambda e: e.matmul(self.ps[7][0:P, 0:1], self.onesPP_f[0:1, :], self.drow[0:1, 0:1], start=True, stop=True))
        for b in range(1, 8):
            self.emit("tensor", None, make=lambda e, b=b: e.matmul(self.ps[7][0:P, b:b + 1], self.onesPP_f[0:1, :], self.ddiff[0:1, b:b + 1], start=True, stop=True))
            self.emit("tensor", None, make=lambda e, b=b: e.matmul(self.ps[7][0:P, 7 + b:8 + b], self.onesPP_f[0:1, :], self.drow[0:1, 8 + b:9 + b], start=True, stop=True))
        self.mark("tensor", "t")
        self.wait("vector", "t")
        self.emit("vector", "v", 1, lambda e: e.tensor_copy(self.bc_sb[:, 0:15], self.ps[7][0:P, 0:15]))
        self.wait("vector", "v")
        self.emit("vector", "v", 1, lambda e: e.tensor_scalar(self.scr_tm[:], self.riota[:], self.bc_sb[:, 0:1], None, ALU.add))
        for b in range(1, 8):
            self.wait("vector", "v")
            self.emit("vector", "v", 1, lambda e, b=b: e.tensor_scalar(self.msk_tm[:], self.riota[:], self.bc_sb[:, 7 + b:8 + b], None, ALU.is_ge))
            self.wait("vector", "v")
            self.emit("vector", "v", 1, lambda e, b=b: e.scalar_tensor_tensor(self.scr_tm[:], self.msk_tm[:], self.bc_sb[:, b:b + 1], self.scr_tm[:], ALU.mult, ALU.add))
        self.wait("vector", "v")
        self.emit("vector", "v", 1, lambda e: e.tensor_scalar_min(self.scr_tm[:], self.scr_tm[:], 32767.0))
        self.wait("vector", "v")
        self.emit("vector", "v", 1, lambda e: e.tensor_copy(self.didx_i[:], self.scr_tm[:]))

        self.emit("vector", "v", 1, lambda e: e.memset(self.carry[:], 0.0))
        for ch in range(8):
            self._emit_onehot_scan(ch)
            self.emit("vector", "v", 1, lambda e: e.scalar_tensor_tensor(self.tmp_em[:], self.loc_em[:], self.om1[:], self.oh_em[:], ALU.add, ALU.mult))
            self.wait("tensor", "v")
            for j in range(4):
                t = ch * 4 + j
                self.emit("tensor", None, make=lambda e, j=j, t=t: e.matmul(
                    self.ps[7][0:P, t:t + 1], self.tmp_em[:, j * 128:(j + 1) * 128], self.ones8_f[:],
                    start=True, stop=True))
            self.mark("tensor", "t")
            self.wait("vector", "t")
        self.emit("vector", "v", 1, lambda e: e.tensor_copy(self.pos_tm[:], self.ps[7][0:P, 0:32]))
        self.wait("vector", "v")
        self.emit("vector", "v", 1, lambda e: e.tensor_scalar(self.msk_tm[:], self.pos_tm[:], 4096.0, None, ALU.is_lt))
        self.wait("vector", "v")
        self.emit("vector", "v", 1, lambda e: e.scalar_tensor_tensor(self.gfin_tm[:], self.gate_tm[:], 0.5, self.msk_tm[:], ALU.mult, ALU.mult))
        self.emit("vector", "v", 1, lambda e: e.scalar_tensor_tensor(self.scr_tm[:], self.e_tm[:], 2048.0, self.pos_tm[:], ALU.mult, ALU.add))
        self.wait("vector", "v")
        self.emit("vector", "v", 1, lambda e: e.tensor_scalar(self.msk_tm[:], self.pos_tm[:], 2048.0, None, ALU.is_ge))
        self.wait("vector", "v")
        self.emit("vector", "v", 1, lambda e: e.scalar_tensor_tensor(self.pos_tm[:], self.msk_tm[:], 14336.0, self.scr_tm[:], ALU.mult, ALU.add))
        self.wait("vector", "v")
        self.emit("vector", "v", 1, lambda e: e.tensor_scalar_min(self.scr_tm[:], self.pos_tm[:], 32767.0))
        self.wait("vector", "v")
        self.emit("vector", "v", 1, lambda e: e.tensor_copy(self.ridx_i[:], self.scr_tm[:]))
        self.wait("vector", "v")

    def emit_dispatch(self, ln_marks):
        self.wait("gpsimd", "cc")
        self.wait("gpsimd", "v")
        self.dst_marks = [None] * TT
        dg_marks = [0] * TT
        for c in range(TT):
            b = c % 2
            if c >= 2:
                self.wait("gpsimd", "dst%d" % b, self.dst_marks[c - 2][1])
            dg_marks[c] = self.emit("gpsimd", "dg%d" % b, 16, lambda e, b=b, c=c: e.indirect_dma_start(
                out=self.retst_sb[:, b, :], out_offset=None,
                in_=self.ag_x, in_offset=IndirectOffsetOnAxis(self.didx_i[:, c:c + 1], axis=0)))
            self.wait("sync", "dg%d" % b, dg_marks[c])
            self.dst_marks[c] = ("dst%d" % b, self.dma("sync", self.disp_tm.rearrange("(c p) f -> p c f", p=P)[:, c, :], self.retst_sb[:, b, :], "dst%d" % b))

    def emit_ffn_mm(self):
        t_marks = [0, 0]
        es_marks = [0, 0]
        for grp in range(8):
            gb = grp % 2
            self.wait("scalar", "t", t_marks[gb])
            for cc in range(4):
                sm, mv = self.dst_marks[grp * 4 + cc]
                self.wait("scalar", sm, mv)
            for fc in range(FC):
                self.dmat("scalar", self.dispfm_sb[:, gb, fc, :],
                          self.disp_tm[grp * 512:(grp + 1) * 512, fc * P:(fc + 1) * P], "ft%d" % gb)
            self.wait("tensor", "ft%d" % gb)
            ev_marks = {}
            for hc in range(HC):
                bank = hc % 2
                if hc >= 2:
                    sem, val = ev_marks[hc - 2]
                    self.wait("tensor", sem, val)
                for k in range(FC):
                    self.emit("tensor", None, make=lambda e, hc=hc, k=k, gb=gb, bank=bank: e.matmul(
                        self.ps[bank][0:P, 0:512], self.w1_sb[:, k, hc * P:(hc + 1) * P],
                        self.dispfm_sb[:, gb, k, :],
                        start=(k == 0), stop=(k == FC - 1)))
                self.mark("tensor", "t")
                if hc % 2 == 0:
                    self.wait("scalar", "t")
                    v = self.emit("scalar", "s", 1, lambda e, hc=hc, bank=bank: e.activation(
                        self.h_sb[:, hc, :], self.ps[bank][0:P, 0:512], AF.Relu, bias=self.b1_sb[:, hc:hc + 1]))
                    ev_marks[hc] = ("s", v)
                else:
                    self.wait("vector", "t")
                    v = self.emit("vector", "v", 1, lambda e, hc=hc, bank=bank: e.scalar_tensor_tensor(
                        self.h_sb[:, hc, :], self.ps[bank][0:P, 0:512], self.b1_sb[:, hc:hc + 1],
                        self.zero_bf[:], ALU.add, ALU.max))
                    ev_marks[hc] = ("v", v)
            self.wait("tensor", "s")
            self.wait("tensor", "v")
            for w in range(2):
                if w == 1:
                    self.wait("tensor", "v")
                for m2 in range(2):
                    mslot = w * 2 + m2
                    for n in range(2):
                        bank = 2 + m2 * 2 + n
                        for k2 in range(HC):
                            self.emit("tensor", None, make=lambda e, mslot=mslot, n=n, k2=k2, bank=bank: e.matmul(
                                self.ps[bank][0:P, 0:512], self.h_sb[:, k2, mslot * P:(mslot + 1) * P],
                                self.w2_sb[:, k2, n * 512:(n + 1) * 512],
                                start=(k2 == 0), stop=False))
                        self.emit("tensor", None, make=lambda e, n=n, bank=bank: e.matmul(
                            self.ps[bank][0:P, 0:512], self.onesPP_b[0:1, :], self.b2_sb[0:1, n * 512:(n + 1) * 512],
                            start=False, stop=True))
                self.mark("tensor", "t")
                self.wait("vector", "t")
                for m2 in range(2):
                    r0 = grp * 512 + (w * 2 + m2) * P
                    self.wait("vector", "es%d" % m2, es_marks[m2])
                    for n in range(2):
                        bank = 2 + m2 * 2 + n
                        self.emit("vector", "v", 1, lambda e, m2=m2, n=n, bank=bank: e.tensor_copy(
                            self.outst_sb[:, m2, n, :], self.ps[bank][0:P, 0:512]))
                    self.wait("sync", "v")
                    es_marks[m2] = self.dma("sync", self.expout[r0:r0 + P, :], self.outst_sb[:, m2, :, :].rearrange("p a b -> p (a b)"), "es%d" % m2)
            t_marks[gb] = self.cnt["t"]
            if grp == 3:
                self.es_half = (es_marks[0], es_marks[1])

    def emit_ffn_return(self, final=False):
        h0, h1 = self.es_half
        self.wait("gpsimd", "es0", h0)
        self.wait("gpsimd", "es1", h1)
        self.emit("gpsimd", "cc", 1, lambda e: e.collective_compute(
            "AllGather", ALU.bypass, replica_groups=[list(range(NCORE))],
            ins=[self.expout[0:2048, :]], outs=[self.ag_ret[0:16384, :]]))
        self.wait("gpsimd", "es0")
        self.wait("gpsimd", "es1")
        self.emit("gpsimd", "cc", 1, lambda e: e.collective_compute(
            "AllGather", ALU.bypass, replica_groups=[list(range(NCORE))],
            ins=[self.expout[2048:4096, :]], outs=[self.ag_ret[16384:32768, :]]))
        self.wait("gpsimd", "cc")
        v_marks = [0] * TT
        st_marks = [0] * TT
        stch = "ys" if final else "rs"
        for c in range(TT):
            b = c % 2
            if c >= 2:
                self.wait("gpsimd", "v", v_marks[c - 2])
                self.wait("sync", stch + str(b), st_marks[c - 2])
            g = self.emit("gpsimd", "rg%d" % b, 16, lambda e, b=b, c=c: e.indirect_dma_start(
                out=self.retst_sb[:, b, :], out_offset=None,
                in_=self.ag_ret, in_offset=IndirectOffsetOnAxis(self.ridx_i[:, c:c + 1], axis=0)))
            ld = self.dma("sync", self.xst_sb[:, b, :], self.resid[:, c, :], "rl%d" % b)
            self.wait("vector", "rg%d" % b, g)
            self.wait("vector", "rl%d" % b, ld)
            v_marks[c] = self.emit("vector", "v", 1, lambda e, b=b, c=c: e.scalar_tensor_tensor(
                self.xst_sb[:, b, :], self.retst_sb[:, b, :], self.gfin_tm[:, c:c + 1],
                self.xst_sb[:, b, :], ALU.mult, ALU.add))
            self.wait("gpsimd" if final else "sync", "v", v_marks[c])
            if final:
                st_marks[c] = self.dma("gpsimd", self.y_out.rearrange("(c p) f -> p c f", p=P)[:, c, :], self.xst_sb[:, b, :], "ys%d" % b)
            else:
                st_marks[c] = self.dma("sync", self.resid[:, c, :], self.xst_sb[:, b, :], "rs%d" % b)

    def emit_ffn(self, final=False):
        ln_marks = self.emit_ln()
        self.emit_gating(ln_marks)
        self.emit_routing()
        self.emit_dispatch(ln_marks)
        self.wait("sync", "ds", self.scat_total)
        self.wait("sync", "bcw", self.cnt["bcw"])
        w1l = self.dma("sync", self.w1_sb[:], self.w1b.rearrange("(k p) h -> p k h", p=P), "wf")
        w2l = self.dma("sync", self.w2_sb[:], self.w2b.rearrange("(k p) m -> p k m", p=P), "wf")
        self.wait("tensor", "wf", w2l)
        self.emit_ffn_mm()
        self.emit_ffn_return(final=final)
        self.barrier()

    def emit_qkv(self, ln_marks):
        self.wait("sync", "bca", self.cnt["bca"])
        wl = self.dma("sync", self.wiaT_sb[:], self.wiaTb.rearrange("(k p) f -> p k f", p=P), "wq")
        self.wait("tensor", "wq", wl)
        t_marks = [0, 0]
        s_marks = {}
        qs_marks = {}
        vev_marks = {}
        vq_marks = {}
        for g in range(8):
            gb = g % 2
            self.wait("scalar", "t", t_marks[gb])
            sm, mv = ln_marks[g]
            self.wait("scalar", sm, mv)
            for fc in range(FC):
                self.dmat("scalar", self.xfm_sb[:, gb, fc, :],
                          self.xln_tm[g * 512:(g + 1) * 512, fc * P:(fc + 1) * P], "xt%d" % gb)
            self.wait("tensor", "xt%d" % gb)
            self.wait("tensor", "v", vev_marks.get(g * 4 - 1, 0))
            for m in range(16):
                gm = g * 16 + m
                bank = m % 4
                self.wait("tensor", "s", s_marks.get(gm - 4, 0))
                for k in range(FC):
                    self.emit("tensor", None, make=lambda e, m=m, k=k, gb=gb, bank=bank: e.matmul(
                        self.ps[bank][0:P, 0:512], self.wiaT_sb[:, k, m * P:(m + 1) * P],
                        self.xfm_sb[:, gb, k, :], start=(k == 0), stop=(k == FC - 1)))
                self.mark("tensor", "t")
                self.wait("scalar", "t")
                eb = m % 2
                self.wait("scalar", "qk%d" % eb, qs_marks.get(gm - 2, 0))
                s_marks[gm] = self.emit("scalar", "s", 1, lambda e, bank=bank, eb=eb: e.copy(self.qkev_sb[:, eb, :], self.ps[bank][0:P, 0:512]))
                self.wait("sync", "s")
                qs_marks[gm] = self.dma("sync", self.qk_fm[m * P:(m + 1) * P, g * 512:(g + 1) * 512], self.qkev_sb[:, eb, :], "qk%d" % eb)
            self.wait("tensor", "s")
            for m in range(4):
                c = g * 4 + m
                gv = g * 4 + m
                vb = m % 2
                bank = 4 + vb
                self.wait("tensor", "v", vev_marks.get(gv - 2, 0))
                self.wait("vector", "vst%d" % vb, vq_marks.get(gv - 2, 0))
                for k in range(FC):
                    self.emit("tensor", None, make=lambda e, m=m, k=k, gb=gb, bank=bank: e.matmul(
                        self.ps[bank][0:P, 0:512], self.xfm_sb[:, gb, k, m * P:(m + 1) * P],
                        self.wiaT_sb[:, k, 2048:2560], start=(k == 0), stop=False))
                self.emit("tensor", None, make=lambda e, bank=bank: e.matmul(
                    self.ps[bank][0:P, 0:512], self.onesPP_b[0:1, :], self.vind_b[0:1, 0:512], start=False, stop=True))
                for k in range(FC):
                    self.emit("tensor", None, make=lambda e, m=m, k=k, gb=gb, bank=bank: e.matmul(
                        self.ps[6 if bank == 4 else 7][0:P, 0:512], self.xfm_sb[:, gb, k, m * P:(m + 1) * P],
                        self.wiaT_sb[:, k, 2560:3072], start=(k == 0), stop=False))
                self.emit("tensor", None, make=lambda e, bank=bank: e.matmul(
                    self.ps[6 if bank == 4 else 7][0:P, 0:512], self.onesPP_b[0:1, :], self.vind_b[0:1, 512:1024], start=False, stop=True))
                for k in range(FC):
                    self.emit("tensor", None, make=lambda e, m=m, k=k, gb=gb, vb=vb: e.matmul(
                        self.ps[2 + vb][0:P, 0:16], self.xfm_sb[:, gb, k, m * P:(m + 1) * P],
                        self.wiaT_sb[:, k, 3072:3088], start=(k == 0), stop=False))
                self.emit("tensor", None, make=lambda e, vb=vb: e.matmul(
                    self.ps[2 + vb][0:P, 0:16], self.onesPP_b[0:1, :], self.vind_b[0:1, 1024:1040], start=False, stop=True))
                self.mark("tensor", "t")
                self.wait("vector", "t")
                self.emit("vector", "v", 1, lambda e, vb=vb, bank=bank: e.tensor_copy(self.vev_sb[:, vb, 0:512], self.ps[bank][0:P, 0:512]))
                self.emit("vector", "v", 1, lambda e, vb=vb, bank=bank: e.tensor_copy(self.vev_sb[:, vb, 512:1024], self.ps[6 if bank == 4 else 7][0:P, 0:512]))
                self.emit("vector", "v", 1, lambda e, vb=vb: e.tensor_copy(self.vev_sb[:, vb, 1024:1040], self.ps[2 + vb][0:P, 0:16]))
                vev_marks[gv] = self.cnt["v"]
                self.wait("sync", "v")
                vq_marks[gv] = self.dma("sync", self.vaug[:, c, :], self.vev_sb[:, vb, :], "vst%d" % vb)
            t_marks[gb] = self.cnt["t"]

    def emit_pairs(self):
        for ch in ("qk0", "qk1", "vst0", "vst1"):
            self.wait("sync", ch, self.cnt[ch])
        t_pair = 0
        self.pending_norm = None
        self.norm_marks = (0, 0)
        self.bc_t_mark = 0
        exp_chunk_mark = [0, 0]
        for pr in range(8):
            NQ = NQ_PAIR[pr]
            sA, sB = SLOPE[2 * pr], SLOPE[2 * pr + 1]
            self.wait("sync", "t", t_pair)
            kl = self.dma("sync", self.kpair_sb[:], self.qk_fm[1024 + pr * 128:1024 + (pr + 1) * 128, :], "pl")
            ql = self.dma("sync", self.qpair_sb[:], self.qk_fm[pr * 128:(pr + 1) * 128, :], "pl")
            vl = self.dma("sync", self.vaug_sb[:], self.vaug[:, :, pr * 130:(pr + 1) * 130], "pl")
            self.wait("tensor", "pl", vl)
            self.wait("scalar", "pl", vl)
            for qi in range(4096 // NQ):
                q0 = qi * NQ
                bp = qi % 2
                khi = (q0 + NQ - 1) // 128
                klo = khi
                while klo > 0 and sB * (q0 - ((klo - 1) * 128 + 127)) <= WIN_T:
                    klo -= 1
                nk = khi - klo + 1
                self.wait("vector", "s", exp_chunk_mark[bp])
                self.emit("vector", "v", 1, lambda e, nk=nk, klo=klo, sA=sA, q0=q0, bp=bp: e.tensor_scalar(
                    self.bias_sb[:, bp, 0, 0:nk], self.riota[:, klo:klo + nk], sA, -sA * q0, ALU.mult, ALU.add))
                bias_mark = self.emit("vector", "v", 1, lambda e, nk=nk, klo=klo, sB=sB, q0=q0, bp=bp: e.tensor_scalar(
                    self.bias_sb[:, bp, 1, 0:nk], self.riota[:, klo:klo + nk], sB, -sB * q0, ALU.mult, ALU.add))
                self.wait("scalar", "v", bias_mark)
                exp_marks = {}
                ctx_marks = {}
                sc_marks = {}

                def emit_scores(idx):
                    ki = klo + idx
                    sb = idx % 2
                    diag = ki * 128 + 127 > q0
                    if idx >= 2:
                        self.wait("tensor", "s", exp_marks[idx - 2])
                    self.emit("tensor", None, make=lambda e, ki=ki, sb=sb, diag=diag, q0=q0, NQ=NQ: e.matmul(
                        self.ps[sb][0:P, 0:NQ], self.kpair_sb[0:64, ki * 128:(ki + 1) * 128],
                        self.qpair_sb[0:64, q0:q0 + NQ], start=True, stop=not diag))
                    self.emit("tensor", None, make=lambda e, ki=ki, sb=sb, diag=diag, q0=q0, NQ=NQ: e.matmul(
                        self.ps[2 + sb][0:P, 0:NQ], self.kpair_sb[64:128, ki * 128:(ki + 1) * 128],
                        self.qpair_sb[64:128, q0:q0 + NQ], start=True, stop=not diag))
                    if diag:
                        soffs = 512 + q0 - ki * 128
                        self.emit("tensor", None, make=lambda e, soffs=soffs, sb=sb, NQ=NQ: e.matmul(
                            self.ps[sb][0:P, 0:NQ], self.ident_b[:], self.stair_b[:, soffs:soffs + NQ],
                            start=False, stop=True))
                        self.emit("tensor", None, make=lambda e, soffs=soffs, sb=sb, NQ=NQ: e.matmul(
                            self.ps[2 + sb][0:P, 0:NQ], self.ident_b[:], self.stair_b[:, soffs:soffs + NQ],
                            start=False, stop=True))
                    self.mark("tensor", "t")
                    sc_marks[idx] = self.cnt["t"]

                def emit_exp(idx):
                    sb = idx % 2
                    eb = idx % 4
                    self.wait("scalar", "t", sc_marks[idx])
                    if idx >= 4:
                        self.wait("scalar", "t", ctx_marks[idx - 4])
                    self.emit("scalar", "s", 1, lambda e, sb=sb, eb=eb, idx=idx, NQ=NQ, bp=bp: e.activation(
                        self.E_sb[:, eb, 0, 0:NQ], self.ps[sb][0:P, 0:NQ], AF.Exp,
                        bias=self.bias_sb[:, bp, 0, idx:idx + 1], scale=0.125))
                    exp_marks[idx] = self.emit("scalar", "s", 1, lambda e, sb=sb, eb=eb, idx=idx, NQ=NQ, bp=bp: e.activation(
                        self.E_sb[:, eb, 1, 0:NQ], self.ps[2 + sb][0:P, 0:NQ], AF.Exp,
                        bias=self.bias_sb[:, bp, 1, idx:idx + 1], scale=0.125))

                def emit_ctx(idx):
                    ki = klo + idx
                    eb = idx % 4
                    first = idx == 0
                    last = idx == nk - 1
                    if first:
                        self.wait("tensor", "s", self.norm_marks[0])
                        self.wait("tensor", "v", self.norm_marks[1])
                    self.wait("tensor", "s", exp_marks[idx])
                    self.emit("tensor", None, make=lambda e, ki=ki, eb=eb, NQ=NQ, first=first, last=last: e.matmul(
                        self.ps[4][0:64, 0:NQ], self.vaug_sb[:, ki, 0:64], self.E_sb[:, eb, 0, 0:NQ],
                        start=first, stop=last))
                    self.emit("tensor", None, make=lambda e, ki=ki, eb=eb, NQ=NQ, first=first, last=last: e.matmul(
                        self.ps[5][64:128, 0:NQ], self.vaug_sb[:, ki, 64:128], self.E_sb[:, eb, 1, 0:NQ],
                        start=first, stop=last))
                    self.emit("tensor", None, make=lambda e, ki=ki, eb=eb, NQ=NQ, first=first, last=last: e.matmul(
                        self.ps[6][0:1, 0:NQ], self.vaug_sb[:, ki, 128:129], self.E_sb[:, eb, 0, 0:NQ],
                        start=first, stop=last))
                    self.emit("tensor", None, make=lambda e, ki=ki, eb=eb, NQ=NQ, first=first, last=last: e.matmul(
                        self.ps[6][64:65, 0:NQ], self.vaug_sb[:, ki, 128:129], self.E_sb[:, eb, 1, 0:NQ],
                        start=first, stop=last))
                    self.mark("tensor", "t")
                    ctx_marks[idx] = self.cnt["t"]

                emit_scores(0)
                emit_exp(0)
                if nk > 1:
                    emit_scores(1)
                    emit_exp(1)
                if self.pending_norm is not None:
                    self.pending_norm()
                    self.pending_norm = None
                if nk > 1:
                    emit_ctx(0)
                for idx in range(2, nk):
                    emit_scores(idx)
                    emit_exp(idx)
                    emit_ctx(idx - 1)
                if nk == 1:
                    emit_ctx(0)
                else:
                    emit_ctx(nk - 1)
                exp_chunk_mark[bp] = exp_marks[nk - 1]

                self.wait("vector", "t")
                self.wait("vector", "t", self.bc_t_mark)
                self.emit("vector", "v", 1, lambda e, NQ=NQ: e.reciprocal(self.rrow_sb[0:1, 0:NQ], self.ps[6][0:1, 0:NQ]))
                self.emit("vector", "v", 1, lambda e, NQ=NQ: e.reciprocal(self.rrow_sb[64:65, 0:NQ], self.ps[6][64:65, 0:NQ]))
                self.wait("vector", "v")
                self.emit("vector", "v", 1, lambda e, NQ=NQ: e.tensor_copy(self.rrowb_sb[0:1, 0:NQ], self.rrow_sb[0:1, 0:NQ]))
                rb_mark = self.emit("vector", "v", 1, lambda e, NQ=NQ: e.tensor_copy(self.rrowb_sb[64:65, 0:NQ], self.rrow_sb[64:65, 0:NQ]))

                def make_norm(pr=pr, q0=q0, NQ=NQ, rb_mark=rb_mark):
                    def norm():
                        self.wait("tensor", "v", rb_mark)
                        self.emit("tensor", None, make=lambda e: e.matmul(
                            self.ps[7][0:64, 0:NQ], self.onesPP_b[0:1, 0:64], self.rrowb_sb[0:1, 0:NQ], start=True, stop=True))
                        self.emit("tensor", "t", 1, lambda e: e.matmul(
                            self.ps[7][64:128, 0:NQ], self.onesPP_b[64:65, 0:64], self.rrowb_sb[64:65, 0:NQ], start=True, stop=True))
                        self.bc_t_mark = self.cnt["t"]
                        self.wait("vector", "t")
                        self.emit("vector", "v", 1, lambda e: e.tensor_copy(self.mtmp_sb[0:64, 0:NQ], self.ps[4][0:64, 0:NQ]))
                        self.emit("vector", "v", 1, lambda e: e.tensor_copy(self.mtmp_sb[64:128, 0:NQ], self.ps[5][64:128, 0:NQ]))
                        self.wait("vector", "v")
                        self.emit("vector", "v", 1, lambda e: e.tensor_mul(
                            self.ctx_sb[0:64, pr, q0:q0 + NQ], self.mtmp_sb[0:64, 0:NQ], self.ps[7][0:64, 0:NQ]))
                        vv = self.emit("vector", "v", 1, lambda e: e.tensor_mul(
                            self.ctx_sb[64:128, pr, q0:q0 + NQ], self.mtmp_sb[64:128, 0:NQ], self.ps[7][64:128, 0:NQ]))
                        self.norm_marks = (self.cnt["s"], vv)
                    return norm
                self.pending_norm = make_norm()
            t_pair = self.cnt["t"]
        if self.pending_norm is not None:
            self.pending_norm()
            self.pending_norm = None

    def emit_proj(self):
        w0 = self.dma("sync", self.wo_sb[:, 0, :, :], self.woTb.rearrange("(k p) f -> p k f", p=P)[:, :, 0:512], "wol")
        w1 = self.dma("sync", self.wo_sb[:, 1, :, :], self.woTb.rearrange("(k p) f -> p k f", p=P)[:, :, 512:1024], "wol")
        self.wait("tensor", "wol", w1)
        st_marks = [0, 0]
        v_marks = [0, 0]
        for m in range(TT):
            mb = m % 2
            self.wait("sync", "pjs%d" % mb, st_marks[mb])
            self.wait("sync", "v", v_marks[mb])
            ld = self.dma("sync", self.prj_sb[:, mb, :], self.resid[:, m, :], "pjl%d" % mb)
            self.wait("tensor", "v", v_marks[mb])
            for n in range(2):
                for k in range(FC):
                    self.emit("tensor", None, make=lambda e, m=m, n=n, k=k, mb=mb: e.matmul(
                        self.ps[mb * 2 + n][0:P, 0:512], self.ctx_sb[:, k, m * P:(m + 1) * P],
                        self.wo_sb[:, n, k, :], start=(k == 0), stop=(k == FC - 1)))
            self.mark("tensor", "t")
            self.wait("vector", "t")
            self.wait("vector", "pjl%d" % mb, ld)
            for n in range(2):
                self.emit("vector", "v", 1, lambda e, mb=mb, n=n: e.tensor_tensor(
                    self.pout_sb[:, mb, n * 512:(n + 1) * 512], self.ps[mb * 2 + n][0:P, 0:512],
                    self.prj_sb[:, mb, n * 512:(n + 1) * 512], ALU.add))
            v_marks[mb] = self.cnt["v"]
            self.wait("sync", "v", v_marks[mb])
            st_marks[mb] = self.dma("sync", self.resid[:, m, :], self.pout_sb[:, mb, :], "pjs%d" % mb)

    def emit_attn(self):
        ln_marks = self.emit_ln()
        self.emit_qkv(ln_marks)
        self.emit_pairs()
        self.barrier()
        self.emit_proj()
        self.barrier()

    def store_y_from_resid(self):
        st_marks = [0, 0]
        for c in range(TT):
            b = c % 2
            self.wait("sync", "ys%d" % b, st_marks[b])
            ld = self.dma("sync", self.xst_sb[:, b, :], self.resid[:, c, :], "syl%d" % b)
            self.wait("gpsimd", "syl%d" % b, ld)
            st_marks[b] = self.dma("gpsimd", self.y_out.rearrange("(c p) f -> p c f", p=P)[:, c, :], self.xst_sb[:, b, :], "ys%d" % b)

    def finish(self):
        self.barrier()

    def dbg_route(self):
        self.dbg("e_tm", self.e_tm[:], [P, TT])
        self.dbg("gate_tm", self.gate_tm[:], [P, TT])
        self.dbg("pos_tm", self.pos_tm[:], [P, TT])
        self.dbg("gfin_tm", self.gfin_tm[:], [P, TT])
        self.dbg("cnt_all", self.cnt_all[:], [8, 8])
        self.dbg("didx", self.didx_i[:], [P, TT], I32)
        self.dbg("dst", self.dst_i[:], [P, TT], I32)
        self.dbg("ridx", self.ridx_i[:], [P, TT], I32)

    def build(self):
        nc = self.nc
        self.declare()
        self.alloc_sbuf()
        self.alloc_psum()
        with ExitStack() as es:
            for n in self.SEMS:
                self.sems[n] = es.enter_context(nc.semaphore(n))
            self.p0_load()
            st = self.stage
            if st == "full":
                pass
            elif st == "full2":
                pass
            self.emit_ffn(final=False)
            self.emit_attn()
            self.emit_ffn(final=True)
            self.finish()

            ops = self.ops
            with nc.Block() as block:
                @block.sync
                def _(e):
                    for f in ops["sync"]:
                        f(e)

                @block.tensor
                def _(e):
                    for f in ops["tensor"]:
                        f(e)

                @block.vector
                def _(e):
                    for f in ops["vector"]:
                        f(e)

                @block.scalar
                def _(e):
                    for f in ops["scalar"]:
                        f(e)

                @block.gpsimd
                def _(e):
                    for f in ops["gpsimd"]:
                        f(e)
        return nc


def host_inputs(inputs):
    x = np.asarray(inputs["L0_attn_x"])
    w_in = np.asarray(inputs["L0_attn_w_in"])
    w_out = np.asarray(inputs["L0_attn_w_out"])
    wg = np.ascontiguousarray(np.asarray(inputs["L0_ffn_wg"]))
    w1 = np.asarray(inputs["L0_ffn_w1"])
    b1 = np.asarray(inputs["L0_ffn_b1"])
    w2 = np.asarray(inputs["L0_ffn_w2"])
    b2 = np.asarray(inputs["L0_ffn_b2"])
    wiaT = np.zeros((DM, WIA_W), dtype=np.float32)
    wiaT[:, 0:1024] = w_in[0:1024].T
    wiaT[:, 1024:2048] = w_in[1024:2048].T
    for pr in range(8):
        base = 2048 + 130 * pr
        wiaT[:, base:base + 64] = w_in[2048 + 128 * pr: 2048 + 128 * pr + 64].T
        wiaT[:, base + 64:base + 128] = w_in[2048 + 128 * pr + 64: 2048 + 128 * pr + 128].T
    woT = np.ascontiguousarray(w_out.T)
    maps = []
    for i in range(NCORE):
        maps.append({
            "x_my": np.ascontiguousarray(x[i]),
            "w1_my": np.ascontiguousarray(w1[i]),
            "w2_my": np.ascontiguousarray(w2[i]),
            "b1_my": np.ascontiguousarray(b1[i].reshape(HC, P).T),
            "b2_my": np.ascontiguousarray(b2[i].reshape(1, DM)),
            "wg": wg,
            "wiaT": wiaT,
            "woT": woT,
        })
    return maps


def kernel(**inputs):
    maps = host_inputs(inputs)
    m = MK(stage="full")
    nc = m.build()
    from concourse.bass_utils import run_bass_kernel_spmd
    res = run_bass_kernel_spmd(nc, maps, core_ids=list(range(NCORE)))
    return np.stack([np.asarray(res.results[c]["y"]).astype(np.float32) for c in range(NCORE)])
